# version 16
# baseline (speedup 1.0000x reference)
import os
import sys

for _p in ("/opt/trn_rl_repo",):
    if os.path.isdir(_p) and _p not in sys.path:
        sys.path.insert(0, _p)

import numpy as np
import ml_dtypes

import concourse.bass as bass
import concourse.mybir as mybir
from concourse.bass_utils import run_bass_kernel_spmd

F32 = mybir.dt.float32
BF16 = mybir.dt.bfloat16
I32 = mybir.dt.int32
AF = mybir.ActivationFunctionType
ALU = mybir.AluOpType
AX = mybir.AxisListType.X

B2 = 2
S = 4096
C = 1024
H = 8
D = 128
HID = 4096
NT = 32
CSET = [0, 1, 2, 4, 8, 16]
OFFS = [0] + [2 ** k for k in range(12)]
ISQ = 0.08838834764831845
EXPB = -30.0
MAGIC = 128.5

DEBUG = False
ENGINES = ("sync", "tensor", "vector", "scalar", "gpsimd")
SY, T, V, A, G = ENGINES


def npairs(t):
    return sum(1 for c in CSET if t - c >= 0)


class Prog:

    def __init__(self, nc):
        self.nc = nc
        self.q = {e: [] for e in ENGINES}
        self.sems = {}
        self.cnt = {}
        self.last_wait = {}

    def sem(self, *names):
        for n in names:
            self.sems[n] = self.nc.alloc_semaphore(n)
            self.cnt[n] = 0

    def op(self, eng, fn=None, wait=(), inc=(), ns=False):
        if eng == "vector":
            wait = list(wait) + [("VS", self.cnt["VS"])]
            if ns:
                inc = list(inc) + [("VS", 1)]
        wl = []
        for s, v in wait:
            v = self.cnt[s] if v is None else v
            k = (eng, s)
            if v <= 0 or self.last_wait.get(k, -1) >= v:
                continue
            self.last_wait[k] = v
            wl.append((s, v))
        il = []
        for s, n in inc:
            self.cnt[s] += n
            il.append((s, n))
        sems = self.sems

        def run(h, wl=wl, il=il, fn=fn):
            for s, v in wl:
                h.wait_ge(sems[s], v)
            ins = fn(h) if fn is not None else None
            if il:
                if ins is None:
                    ins = h.nop()
                for s, n in il:
                    ins = ins.then_inc(sems[s], n)
        self.q[eng].append(run)

    def barrier(self):
        for e in ENGINES:
            if e == SY:
                self.op(e, lambda h: h.nop(),
                        wait=[(s, None) for s in ("dsp", "dst", "dgp", "dso")],
                        inc=[("stg", 1)])
            else:
                self.op(e, lambda h: h.drain(), inc=[("stg", 1)])
        cur = self.cnt["stg"]
        for e in ENGINES:
            self.op(e, wait=[("stg", cur)])


RACEDET = False


def build_nc():
    nc = bass.Bass("TRN2", target_bir_lowering=False,
                   detect_race_conditions=RACEDET)

    x_d = nc.dram_tensor("x", [B2, S, C], F32, kind="ExternalInput").ap()
    wqkv_d = nc.dram_tensor("wqkv", [C, 3 * C], F32, kind="ExternalInput").ap()
    wo_d = nc.dram_tensor("wo", [C, C], F32, kind="ExternalInput").ap()
    convw_d = nc.dram_tensor("convw", [C, 7], F32, kind="ExternalInput").ap()
    w1_d = nc.dram_tensor("w1", [HID, C], F32, kind="ExternalInput").ap()
    w2_d = nc.dram_tensor("w2", [C, HID], F32, kind="ExternalInput").ap()
    out_d = nc.dram_tensor("out", [B2, S, C], F32, kind="ExternalOutput").ap()

    _k = "ExternalOutput" if DEBUG else "Internal"
    w1qt_d = nc.dram_tensor("w1qt_s", [C, HID], BF16, kind=_k).ap()
    w2qt_d = nc.dram_tensor("w2qt_s", [HID, C], BF16, kind=_k).ap()
    xq_d = nc.dram_tensor("xq_s", [B2, S, C], BF16, kind=_k).ap()
    mix_d = nc.dram_tensor("mix_s", [B2, C, S], BF16, kind=_k).ap()

    wl_ = np.arange(128)[:, None]
    ql_ = np.arange(128)[None, :]
    masks_np = np.stack(
        [np.isin(128 * c + ql_ - wl_, OFFS).astype(np.float32) for c in CSET])
    masks_i = nc.inline_tensor(
        masks_np.transpose(1, 0, 2).astype(ml_dtypes.bfloat16).copy(), "masks_c")
    ident_i = nc.inline_tensor(np.eye(128, dtype=ml_dtypes.bfloat16), "ident_c")

    BASE = 16512

    def alloc(name, shape, dtype, offset=0):
        assert offset + BASE + int(np.prod(shape[1:])) * mybir.dt.size(dtype)             <= nc.sbuf_top, (name, offset)
        return nc.alloc_sbuf_tensor_at(name, list(shape), dtype, offset=offset + BASE)
    off = [0]

    def sal(name, shape, dtype, at=None):
        nbytes = int(np.prod(shape[1:])) * mybir.dt.size(dtype)
        if at is None:
            at = off[0]
        t = alloc(name, list(shape), dtype, offset=at)
        off[0] = max(off[0], at + ((nbytes + 31) // 32) * 32)
        return t

    masks_s = sal("masks_s", [128, 6, 128], BF16)
    ones_b = sal("ones_b", [128, 128], BF16)
    ones_f = sal("ones_f", [128, 128], F32)
    ident_s = sal("ident_s", [128, 128], BF16)
    cwf_s = sal("cwf_s", [128, H, 7], F32)
    diag_s = sal("diag_s", [128, 7, 128], BF16)
    cc = sal("cc", [128, 16], F32)
    tmpc = sal("tmpc", [128, 16], F32)
    red32 = sal("red32", [128, 32], F32)
    rstd2 = sal("rstd2", [128, 64], F32)
    nmr2 = sal("nmr2", [128, 64], F32)
    sxr2 = sal("sxr2", [128, 64], F32)
    s1c = sal("s1c", [128, 2], F32)
    assert off[0] <= 5664, off[0]
    K8 = 5664

    wfb = alloc("wfb", [128, 33, 1024], F32, offset=K8)
    wqb_s = alloc("wqb_s", [128, 32, 1024], BF16, offset=K8 + 135168)
    tstg = alloc("tstg", [128, 2, 8, 128], BF16, offset=K8 + 135168 + 65536)
    c2q = alloc("c2q", [128, 1024], BF16, offset=K8 + 135168 + 65536 + 4096)
    assert K8 + 135168 + 65536 + 4096 + 2048 <= 212832

    xn_fm = alloc("xn_fm", [128, NT, 8, 128], BF16, offset=K8)
    AT = K8 + 65536
    q_s = alloc("q_s", [128, S], BF16, offset=AT)
    k_s = alloc("k_s", [128, S], BF16, offset=AT + 8192)
    vpad = alloc("vpad", [128, 64 + S], BF16, offset=AT + 16384)
    vsT = alloc("vsT", [128, NT, 128], BF16, offset=AT + 24704)
    fq_s = alloc("fq_s", [128, S], BF16, offset=AT + 32896)
    fk_s = alloc("fk_s", [128, S], BF16, offset=AT + 41088)
    fksT = alloc("fksT", [128, NT, 128], BF16, offset=AT + 49280)
    tmpz = alloc("tmpz", [128, S], BF16, offset=AT + 57472)
    mixf = alloc("mixf", [128, S], F32, offset=AT + 65664)
    rz_s = alloc("rz_s", [128, S], F32, offset=AT + 82048)
    wexp = alloc("wexp", [128, 2, 768], BF16, offset=AT + 98432)
    rden = alloc("rden", [128, 2, 128], F32, offset=AT + 101504)
    wqkv_s = alloc("wqkv_s", [128, 2, 3, 8, 128], BF16, offset=AT + 102528)
    kv_b = alloc("kv_b", [128, 128], BF16, offset=AT + 114816)
    mixb = alloc("mixb", [128, S], BF16, offset=AT + 115104)
    wexp2 = alloc("wexp2", [128, 2, 768], BF16, offset=AT + 123296)
    zscr = alloc("zscr", [128, 512], F32, offset=AT + 126368)
    wexp4 = alloc("wexp4", [128, 4, 768], BF16, offset=AT + 128416)
    assert AT + 134560 <= 212832
    x4_s = alloc("x4_s", [128, 2, 4, 1024], F32, offset=AT)
    sqs_s = alloc("sqs_s", [128, 1024], F32, offset=AT + 32896)
    xnf_s = alloc("xnf_s", [128, 1024], F32, offset=AT + 36992)
    xnb_s = alloc("xnb_s", [128, 2, 1024], BF16, offset=AT + 41088)
    xqi_s = alloc("xqi_s", [128, 1024], F32, offset=AT + 45184)
    xqb_s = alloc("xqb_s", [128, 2, 1024], BF16, offset=AT + 49280)
    xqt_s = alloc("xqt_s", [128, 2, 8, 128], BF16, offset=AT + 53376)

    w1t_s = alloc("w1t_s", [128, 8, HID], BF16, offset=K8)
    w2t_s = alloc("w2t_s", [128, 32, 1024], BF16, offset=AT)
    wo_s = alloc("wo_s", [128, 8, 1024], BF16, offset=AT + 65664)
    g1_s = alloc("g1_s", [128, HID], F32, offset=AT + 82048)
    qscr = alloc("qscr", [128, 2048], F32, offset=AT + 98432)
    hq_q = alloc("hq_q", [128, 2, 512], BF16, offset=AT + 102528)
    hqhm = alloc("hqhm", [128, 32, 128], BF16, offset=AT + 106624)
    P3L = AT + 114816
    xqf_s = alloc("xqf_s", [128, 2, 8, 128], BF16, offset=P3L)
    mxs_s = alloc("mxs_s", [128, 2, 8, 128], BF16, offset=P3L + 4096)
    xt_s = alloc("xt_s", [128, 1024], F32, offset=P3L + 8192)
    st_s = alloc("st_s", [128, 1024], F32, offset=P3L + 12288)
    gt_s = alloc("gt_s", [128, 2, 1024], F32, offset=P3L + 16384)
    assert P3L + 24576 <= 212832, P3L + 24576

    ps = nc.alloc_psum_tensor("ps", [128, 8, 512], F32)
    psf = ps.rearrange("p b n -> p (b n)")

    def bank(i, a=0, n=512):
        return ps[:, i, a:a + n]

    p = Prog(nc)
    p.sem("stg", "dsp", "dgp", "dst", "dso", "PE", "DV", "AC", "VS")
    c_ = p.cnt

    bfree = {i: None for i in range(8)}

    def bwait(*bks):
        return [bfree[b] for b in set(bks) if bfree[b] is not None]

    def dma(dst, src, inc="dsp", eng=SY, wait=()):
        p.op(eng, lambda h, d=dst, s=src: h.dma_start(out=d, in_=s),
             wait=wait, inc=[(inc, 16)])

    def dmat(dst, src, wait=()):
        p.op(SY, lambda h, d=dst, s=src: h.dma_start(out=d, in_=s, transpose=True),
             wait=wait, inc=[("dsp", 16)])

    def emit_floor(u, r1i, r1f, cbuf, out_ap, wait_last=(), inc_last=()):
        p.op(V, lambda h: h.tensor_copy(out=r1i, in_=u))
        p.op(V, lambda h: h.tensor_copy(out=r1f, in_=r1i))
        p.op(V, lambda h: h.tensor_tensor(out=cbuf, in0=r1f, in1=u, op=ALU.is_gt))
        p.op(V, lambda h: h.scalar_tensor_tensor(
            out=out_ap, in0=r1f, scalar=-128.0, in1=cbuf,
            op0=ALU.add, op1=ALU.subtract), wait=wait_last, inc=inc_last)

    dma(masks_s[:], masks_i.ap())
    dma(ident_s[:], ident_i.ap())
    dma(cwf_s[:], convw_d.rearrange("(h p) t -> p h t", p=128))
    for dst, val in ((ones_b[:], 1.0), (ones_f[:], 1.0), (cc[:, 0:1], EXPB),
                     (cc[:, 1:2], ISQ), (cc[:, 2:3], 1e-5), (cc[:, 3:4], 1e-6)):
        p.op(V, lambda h, d=dst, v=val: h.memset(d, v), inc=[("DV", 1)])

    prev_dv = None
    for wsrc, wtd, nrc, ncol, sidx in (
            (w1_d, w1qt_d, 32, 1024, 0), (w2_d, w2qt_d, 8, HID, 1)):
        if ncol == 1024:
            src_r = wsrc.rearrange("(rc p) i -> p rc i", p=128)
            for i in range(4):
                dma(wfb[:, i * 8:(i + 1) * 8, :], src_r[:, i * 8:(i + 1) * 8, :],
                    inc="dsp" if i % 2 == 0 else "dgp",
                    eng=SY if i % 2 == 0 else G)
        else:
            src_r = wsrc.rearrange("(rc p) (q i) -> p rc q i", p=128, i=1024)
            for i in range(4):
                dma(wfb[:, 0:32, :].rearrange("p (rc q) i -> p rc q i", q=4)
                    [:, i * 2:(i + 1) * 2, :, :],
                    src_r[:, i * 2:(i + 1) * 2, :, :],
                    inc="dsp" if i % 2 == 0 else "dgp",
                    eng=SY if i % 2 == 0 else G,
                    wait=[("DV", prev_dv[8 * (i + 1) - 1])])
        p.op(V, lambda h: h.tensor_reduce(
            out=red32[:, 0:32], in_=wfb[:, 0:32, :], axis=AX, op=ALU.add,
            apply_absolute_value=True),
            wait=[("dsp", None), ("dgp", None)], ns=True)
        p.op(V, lambda h: h.tensor_reduce(
            out=tmpc[:, 0:1], in_=red32[:, 0:32], axis=AX, op=ALU.add),
            inc=[("DV", 1)])
        p.op(T, lambda h: h.matmul(bank(0, 0, 1), ones_f[:], tmpc[:, 0:1],
                                   start=True, stop=True),
             wait=[("DV", None)] + bwait(0), inc=[("PE", 1)])
        sc = s1c[:, sidx:sidx + 1]
        p.op(V, lambda h, sc=sc: h.tensor_scalar(
            out=sc, in0=bank(0, 0, 1), scalar1=1.0 / (32 * 128 * 1024),
            scalar2=1e-5, op0=ALU.mult, op1=ALU.max), wait=[("PE", None)], ns=True)
        bfree[0] = ("DV", c_["DV"] + 1)
        p.op(V, lambda h, sc=sc: h.reciprocal(out=tmpc[:, 1:2], in_=sc),
             ns=True)
        st_ms = {0: 0, 1: 0}
        wtr = wtd.rearrange("(ic p) o -> p ic o", p=128)
        scr = wfb[:, 32, :]
        prev_dv = []
        for b in range(32):
            wqv = wqb_s[:, b, :]
            p.op(V, lambda h, b=b: h.tensor_scalar(
                out=scr, in0=wfb[:, b, :], scalar1=tmpc[:, 1:2],
                scalar2=None, op0=ALU.mult))
            p.op(V, lambda h: h.tensor_scalar(
                out=c2q[:], in0=scr, scalar1=-0.5,
                scalar2=None, op0=ALU.is_lt))
            p.op(V, lambda h, b=b: h.scalar_tensor_tensor(
                out=wqb_s[:, b, :], in0=scr, scalar=0.5, in1=c2q[:],
                op0=ALU.is_gt, op1=ALU.subtract), inc=[("DV", 1)])
            prev_dv.append(c_["DV"])
            rc, qq = (b, 0) if ncol == 1024 else (b // 4, b % 4)
            par = b % 2
            tv = tstg[:, par, :, :]
            dmat(tv, wqv, wait=[("DV", None), ("dst", st_ms[par])])
            dst = wtr[:, qq * 8:(qq + 1) * 8, rc * 128:(rc + 1) * 128]
            dma(dst, tv, inc="dst", wait=[("dsp", None)])
            st_ms[par] = c_["dst"]

    p.barrier()
    bfree = {i: None for i in range(8)}

    w_ms = {0: 0, 1: 0}
    for sq in range(B2):
        col0 = sq * 32
        xr = x_d[sq].rearrange("(n p) c -> p n c", p=128)
        xqr = xq_d[sq].rearrange("(n p) c -> p n c", p=128)

        grp_ms = {0: (0, 0), 1: (0, 0)}
        xqst_ms = {0: 0, 1: 0}
        xnbdm_ms = {0: 0, 1: 0}
        for grp in range(8):
            gp = grp % 2
            dma(x4_s[:, gp, :, :], xr[:, grp * 4:grp * 4 + 4, :],
                inc="dgp", eng=G,
                wait=[("DV", grp_ms[gp][0]), ("AC", grp_ms[gp][1])])
            for j in range(4):
                t = grp * 4 + j
                xt = x4_s[:, gp, j, :]
                scol = col0 + t
                p.op(V, lambda h, xt=xt: h.tensor_reduce(
                    out=tmpc[:, 2:3], in_=xt, axis=AX, op=ALU.add),
                    wait=[("dgp", None)], ns=True)
                p.op(A, lambda h, xt=xt: h.activation(
                    out=sqs_s[:], in_=xt, func=AF.Square, accum_out=tmpc[:, 3:4]),
                    wait=[("dgp", None), ("DV", None)], inc=[("AC", 1)])
                p.op(V, lambda h: h.tensor_scalar(
                    out=tmpc[:, 4:5], in0=tmpc[:, 2:3], scalar1=1.0 / C,
                    scalar2=None, op0=ALU.mult), wait=[("AC", None)], ns=True)
                p.op(V, lambda h: h.tensor_scalar(
                    out=tmpc[:, 5:6], in0=tmpc[:, 3:4], scalar1=1.0 / C,
                    scalar2=None, op0=ALU.mult), ns=True)
                p.op(V, lambda h: h.tensor_tensor(
                    out=tmpc[:, 6:7], in0=tmpc[:, 4:5], in1=tmpc[:, 4:5],
                    op=ALU.mult), ns=True)
                p.op(V, lambda h: h.tensor_tensor(
                    out=tmpc[:, 7:8], in0=tmpc[:, 5:6], in1=tmpc[:, 6:7],
                    op=ALU.subtract), inc=[("DV", 1)])
                p.op(A, lambda h: h.activation(
                    out=tmpc[:, 8:9], in_=tmpc[:, 7:8], func=AF.Sqrt,
                    bias=cc[:, 2:3], scale=1.0),
                    wait=[("DV", None)], inc=[("AC", 1)])
                p.op(V, lambda h, scol=scol: h.reciprocal(
                    out=rstd2[:, scol:scol + 1], in_=tmpc[:, 8:9]),
                    wait=[("AC", None)], ns=True)
                p.op(V, lambda h, scol=scol: h.tensor_tensor(
                    out=tmpc[:, 9:10], in0=tmpc[:, 4:5],
                    in1=rstd2[:, scol:scol + 1], op=ALU.mult), ns=True)
                p.op(V, lambda h, scol=scol: h.tensor_scalar(
                    out=nmr2[:, scol:scol + 1], in0=tmpc[:, 9:10], scalar1=-1.0,
                    scalar2=None, op0=ALU.mult), ns=True)
                p.op(V, lambda h, xt=xt, scol=scol: h.tensor_scalar(
                    out=xnf_s[:], in0=xt,
                    scalar1=rstd2[:, scol:scol + 1], scalar2=nmr2[:, scol:scol + 1],
                    op0=ALU.mult, op1=ALU.add), inc=[("DV", 1)])
                p.op(A, lambda h, xt=xt, j=j, scol=scol: h.activation(
                    out=xnb_s[:, j % 2, :], in_=xt, func=AF.Identity,
                    scale=rstd2[:, scol:scol + 1], bias=nmr2[:, scol:scol + 1]),
                    wait=[("dsp", xnbdm_ms[j % 2]), ("DV", None)],
                    inc=[("AC", 1)])
                dmat(xn_fm[:, t, :, :], xnb_s[:, j % 2, :], wait=[("AC", None)])
                xnbdm_ms[j % 2] = c_["dsp"]
                p.op(V, lambda h: h.tensor_reduce(
                    out=tmpc[:, 10:11], in_=xnf_s[:], axis=AX, op=ALU.max,
                    apply_absolute_value=True), ns=True)
                p.op(V, lambda h: h.tensor_scalar(
                    out=tmpc[:, 11:12], in0=tmpc[:, 10:11], scalar1=1e-5,
                    scalar2=None, op0=ALU.max), ns=True)
                p.op(V, lambda h: h.reciprocal(
                    out=tmpc[:, 12:13], in_=tmpc[:, 11:12]), ns=True)
                p.op(V, lambda h: h.tensor_scalar(
                    out=tmpc[:, 13:14], in0=tmpc[:, 12:13], scalar1=127.0,
                    scalar2=None, op0=ALU.mult), ns=True)
                p.op(V, lambda h, scol=scol: h.tensor_scalar(
                    out=sxr2[:, scol:scol + 1], in0=tmpc[:, 11:12],
                    scalar1=s1c[:, 0:1], scalar2=1.0 / 127.0,
                    op0=ALU.mult, op1=ALU.mult), ns=True)
                p.op(V, lambda h: h.tensor_scalar(
                    out=xqi_s[:], in0=xnf_s[:], scalar1=tmpc[:, 13:14],
                    scalar2=None, op0=ALU.mult))
                p.op(V, lambda h: h.tensor_copy(
                    out=sqs_s[:].bitcast(I32), in_=xqi_s[:]))
                p.op(V, lambda h, j=j: h.tensor_copy(
                    out=xqb_s[:, j % 2, :], in_=sqs_s[:].bitcast(I32)),
                    wait=[("dst", xqst_ms[j % 2])], inc=[("DV", 1)])
                dma(xqr[:, t, :], xqb_s[:, j % 2, :], inc="dst",
                    wait=[("DV", None)])
                xqst_ms[j % 2] = c_["dst"]
            grp_ms[gp] = (c_["DV"], c_["AC"])
        p.barrier()
        bfree = {i: None for i in range(8)}

        p.op(V, lambda h: h.memset(vpad[:, 58:64], 0.0), inc=[("DV", 1)])
        conv_pe_prev = 0
        kv_pe = 0
        low_pe = 0
        mixb_ms = 0
        mixst_ms = 0
        trans_ms = 0
        vsT_ms = 0
        qk_ms = 0
        mixf_ac = 0
        tmz_ms = 0
        eqms = {}
        ekms = {}
        fqms = {}
        for hh in range(H + 1):
            hv = hh - 1
            if hh < H:
                hp = hh % 2
                for pi in range(3):
                    src = wqkv_d[:, pi * C + hh * D: pi * C + hh * D + D]
                    dma(wqkv_s[:, hp, pi, :, :],
                        src.rearrange("(ic p) o -> p ic o", p=128),
                        inc="dgp", eng=G, wait=[("PE", w_ms[hp])])
                for tap in range(7):
                    p.op(V, lambda h, tap=tap, hh=hh: h.tensor_scalar(
                        out=diag_s[:, tap, :], in0=ident_s[:],
                        scalar1=cwf_s[:, hh, tap:tap + 1], scalar2=None,
                        op0=ALU.mult),
                        wait=[("PE", conv_pe_prev),
                              ("dsp", 48 if hh == 0 and sq == 0 else 0)],
                        inc=[("DV", 1)] if tap == 6 else ())
            if hv >= 0:
                for ch in range(NT):
                    p.op(T, lambda h, ch=ch: h.matmul(
                        bank(0, 0, 128), fksT[:, ch, :], vsT[:, ch, :],
                        start=(ch == 0), stop=(ch == NT - 1)),
                        wait=([("dsp", trans_ms)] + bwait(0)
                              if ch == 0 else ()),
                        inc=[("PE", 1)] if ch == NT - 1 else ())
                kv_pe = c_["PE"]
                p.op(V, lambda h: h.tensor_copy(out=kv_b[:], in_=bank(0, 0, 128)),
                     wait=[("PE", kv_pe), ("PE", low_pe)], inc=[("DV", 1)])
                bfree[0] = ("DV", c_["DV"])
            if hv >= 0:
                p.op(V, lambda h: h.tensor_scalar(
                    out=tmpz[:], in0=fq_s[:], scalar1=tmpc[:, 13:14],
                    scalar2=None, op0=ALU.mult),
                    wait=[("PE", low_pe)], inc=[("DV", 1)])
                tmz_ms = c_["DV"]
                for tc in range(8):
                    bk = 2 + tc % 2
                    p.op(T, lambda h, tc=tc, bk=bk: h.matmul(
                        bank(bk), ones_b[:], tmpz[:, tc * 512:(tc + 1) * 512],
                        start=True, stop=True),
                        wait=[("DV", tmz_ms)] + bwait(bk) if tc < 2
                        else bwait(bk),
                        inc=[("PE", 1)])
                    p.op(V, lambda h, tc=tc, bk=bk: h.tensor_scalar(
                        out=zscr[:], in0=bank(bk),
                        scalar1=1e-6, scalar2=None, op0=ALU.add),
                        wait=[("PE", None)])
                    p.op(V, lambda h, tc=tc: h.reciprocal(
                        out=rz_s[:, tc * 512:(tc + 1) * 512],
                        in_=zscr[:]), inc=[("DV", 1)])
                    bfree[bk] = ("DV", c_["DV"])
                p.op(V, lambda h: h.tensor_tensor(
                    out=tmpz[:], in0=fq_s[:], in1=rz_s[:], op=ALU.mult),
                    wait=[("PE", c_["PE"])], inc=[("DV", 1)])
                tz2_ms = c_["DV"]
            if hh < H:
                for pi, dstb in ((0, q_s), (1, k_s), (2, vpad)):
                    for tc in range(8):
                        bk = tc % 2
                        for kc in range(8):
                            p.op(T, lambda h, pi=pi, tc=tc, kc=kc, bk=bk,
                                 hp=hp: h.matmul(
                                     bank(bk), wqkv_s[:, hp, pi, kc, :],
                                     xn_fm[:, 4 * tc:4 * tc + 4, kc, :],
                                     start=(kc == 0), stop=(kc == 7)),
                                 wait=([("dgp", None)] + bwait(bk)
                                       if kc == 0 else ()),
                                 inc=[("PE", 1)] if kc == 7 else ())
                        o0 = 64 + tc * 512 if pi == 2 else tc * 512
                        p.op(A, lambda h, dstb=dstb, o0=o0, bk=bk: h.activation(
                            out=dstb[:, o0:o0 + 512], in_=bank(bk),
                            func=AF.Copy),
                            wait=[("PE", None)], inc=[("AC", 1)])
                        bfree[bk] = ("AC", c_["AC"])
                w_ms[hp] = c_["PE"]
                dmat(vsT[:], vpad[:, 64:64 + S],
                     wait=[("AC", None), ("PE", kv_pe)])
                vsT_ms = c_["dsp"]
                qk_ms = c_["AC"]
            if hv >= 0:
                for tc in range(8):
                    bk = 4 + tc % 2
                    p.op(T, lambda h, tc=tc, bk=bk: h.matmul(
                        bank(bk), kv_b[:], tmpz[:, tc * 512:(tc + 1) * 512],
                        start=True, stop=True),
                        wait=[("DV", tz2_ms)] + bwait(bk) if tc < 2
                        else bwait(bk),
                        inc=[("PE", 1)])
                    p.op(V, lambda h, tc=tc, bk=bk: h.scalar_tensor_tensor(
                        out=mixb[:, tc * 512:(tc + 1) * 512], in0=bank(bk),
                        scalar=1.0, in1=mixf[:, tc * 512:(tc + 1) * 512],
                        op0=ALU.mult, op1=ALU.add),
                        wait=[("PE", None)]
                        + ([("dst", mixst_ms), ("AC", mixf_ac)]
                           if tc == 0 else []),
                        inc=[("DV", 1)])
                    bfree[bk] = ("DV", c_["DV"])
                low_pe = c_["PE"]
                mixb_ms = c_["DV"]
                dma(mix_d[sq, hv * D:(hv + 1) * D, :], mixb[:], inc="dst",
                    wait=[("DV", None)])
                mixst_ms = c_["dst"]
            if hh < H:
                score_ms = {}
                exp_ms = {}
                mask_msd = {}
                denom_ms = {}
                norm_msd = {}
                tele_ms = {}
                low_pe0 = low_pe
                kv_pe0 = kv_pe

                def pc_eq(ck):
                    sl = slice(ck * 1024, (ck + 1) * 1024)
                    p.op(A, lambda h, sl=sl: h.activation(
                        out=tmpz[:, sl], in_=q_s[:, sl], func=AF.Exp),
                        wait=[("PE", low_pe0)] if ck == 0 else (),
                        inc=[("AC", 1)])
                    eqms[ck] = c_["AC"]

                def pc_fq(ck):
                    sl = slice(ck * 1024, (ck + 1) * 1024)
                    p.op(V, lambda h, sl=sl: h.scalar_tensor_tensor(
                        out=fq_s[:, sl], in0=q_s[:, sl], scalar=1.0,
                        in1=tmpz[:, sl], op0=ALU.add, op1=ALU.min),
                        wait=[("AC", eqms[ck])], inc=[("DV", 1)])
                    fqms[ck] = c_["DV"]

                def pc_ek(ck):
                    sl = slice(ck * 1024, (ck + 1) * 1024)
                    p.op(A, lambda h, sl=sl: h.activation(
                        out=tmpz[:, sl], in_=k_s[:, sl], func=AF.Exp),
                        wait=[("DV", fqms[ck])], inc=[("AC", 1)])
                    ekms[ck] = c_["AC"]
                    p.op(V, lambda h, sl=sl: h.scalar_tensor_tensor(
                        out=fk_s[:, sl], in0=k_s[:, sl], scalar=1.0,
                        in1=tmpz[:, sl], op0=ALU.add, op1=ALU.min),
                        wait=[("AC", ekms[ck])], inc=[("DV", 1)])
                    dmat(fksT[:, ck * 8:(ck + 1) * 8, :], fk_s[:, sl],
                         wait=[("DV", None)]
                         + ([("PE", kv_pe0)] if ck == 0 else []))

                def pc_fksum():
                    for ck in range(4):
                        sl = slice(ck * 1024, (ck + 1) * 1024)
                        p.op(V, lambda h, ck=ck, sl=sl: h.tensor_reduce(
                            out=red32[:, 16 + ck:17 + ck], in_=fk_s[:, sl],
                            axis=AX, op=ALU.add), ns=True)
                    p.op(V, lambda h: h.tensor_reduce(
                        out=tmpc[:, 13:14], in_=red32[:, 16:20], axis=AX,
                        op=ALU.add), ns=True)

                pieces = ([lambda ck=ck: pc_eq(ck) for ck in range(4)]
                          + [lambda ck=ck: pc_fq(ck) for ck in range(4)]
                          + [lambda ck=ck: pc_ek(ck) for ck in range(4)]
                          + [pc_fksum])
                for it in range(NT + 2):
                    t, u, v = it, it - 1, it - 2
                    if t < NT:
                        par = t % 2
                        np_t = npairs(t)
                        sb = 2 + 2 * par
                        nsb = min(np_t, 4)
                        for ci in range(np_t):
                            w = t - CSET[ci]
                            bnk = sb if ci < 4 else sb + 1
                            a0 = (ci % 4) * 128
                            first = (ci == 0) or (ci == 4)
                            last = (ci == nsb - 1) or (ci == np_t - 1)
                            p.op(T, lambda h, w=w, t=t, bnk=bnk, a0=a0,
                                 f=first, l=last: h.matmul(
                                     bank(bnk, a0, 128),
                                     k_s[:, w * 128:(w + 1) * 128],
                                     q_s[:, t * 128:(t + 1) * 128],
                                     start=f, stop=l),
                                 wait=(([("AC", qk_ms)] if t < 2 else [])
                                       + bwait(sb, sb + 1)) if ci == 0 else (),
                                 inc=[("PE", 1)] if ci == np_t - 1 else ())
                        score_ms[t] = c_["PE"]
                    if 0 <= u < NT:
                        par_u = u % 2
                        np_u = npairs(u)
                        sbu = 2 + 2 * par_u
                        for ci in range(np_u):
                            p.op(T, lambda h, ci=ci, par_u=par_u, sbu=sbu,
                                 np_u=np_u: h.matmul(
                                     bank(sbu + 1, 256, 128), ones_b[:],
                                     wexp2[:, par_u, ci * 128:(ci + 1) * 128],
                                     start=(ci == 0), stop=(ci == np_u - 1)),
                                 wait=([("DV", mask_msd[u])] + bwait(sbu + 1)
                                       if ci == 0 else ()),
                                 inc=[("PE", 1)] if ci == np_u - 1 else ())
                        denom_ms[u] = c_["PE"]
                    if 0 <= v < NT:
                        par_v = v % 2
                        np_v = npairs(v)
                        dtv = 6 + par_v
                        for ci in range(np_v):
                            w = v - CSET[ci]
                            p.op(T, lambda h, ci=ci, w=w, v=v,
                                 dtv=dtv: h.matmul(
                                     bank(dtv, 0, 128), vsT[:, w, :],
                                     wexp4[:, v % 4, ci * 128:(ci + 1) * 128],
                                     start=(ci == 0), stop=False),
                                 wait=[("DV", norm_msd[v]), ("dsp", vsT_ms)]
                                 + bwait(dtv) if ci == 0 else ())
                        for tap in range(7):
                            p.op(T, lambda h, tap=tap, v=v, dtv=dtv: h.matmul(
                                bank(dtv, 0, 128), diag_s[:, tap, :],
                                vpad[:, 58 + tap + v * 128:
                                     58 + tap + v * 128 + 128],
                                start=False, stop=(tap == 6)),
                                inc=[("PE", 1)] if tap == 6 else ())
                        tele_ms[v] = c_["PE"]
                    if t < NT:
                        p.op(A, lambda h, np_t=np_t, sb=sb, t=t: h.activation(
                            out=wexp4[:, t % 4, 0:np_t * 128],
                            in_=psf[:, sb * 512: sb * 512 + np_t * 128],
                            func=AF.Exp, bias=cc[:, 0:1], scale=cc[:, 1:2]),
                            wait=[("PE", score_ms[t]),
                                  ("PE", tele_ms.get(t - 4, conv_pe_prev))],
                            inc=[("AC", 1)])
                        exp_ms[t] = c_["AC"]
                        bfree[sb] = ("AC", exp_ms[t])
                    if 0 <= v < NT:
                        p.op(A, lambda h, v=v, dtv=dtv: h.activation(
                            out=mixf[:, v * 128:(v + 1) * 128],
                            in_=bank(dtv, 0, 128), func=AF.Copy),
                            wait=[("PE", tele_ms[v])]
                            + ([("DV", mixb_ms)] if v == 0 else []),
                            inc=[("AC", 1)])
                        bfree[dtv] = ("AC", c_["AC"])
                    if t < NT:
                        p.op(V, lambda h, np_t=np_t, t=t: h.tensor_tensor(
                            out=wexp2[:, t % 2, 0:np_t * 128],
                            in0=wexp4[:, t % 4, 0:np_t * 128],
                            in1=masks_s[:, 0:np_t, :]
                            .rearrange("p a b -> p (a b)"),
                            op=ALU.mult),
                            wait=[("AC", exp_ms[t]),
                                  ("PE", denom_ms.get(t - 2, conv_pe_prev))],
                            inc=[("DV", 1)])
                        mask_msd[t] = c_["DV"]
                    if 0 <= u < NT:
                        p.op(V, lambda h, par_u=par_u, sbu=sbu: h.reciprocal(
                            out=rden[:, par_u, :],
                            in_=bank(sbu + 1, 256, 128)),
                            wait=[("PE", denom_ms[u])], inc=[("DV", 1)])
                        bfree[sbu + 1] = ("DV", c_["DV"])
                        for ci in range(np_u):
                            p.op(V, lambda h, ci=ci, u=u,
                                 par_u=par_u: h.tensor_tensor(
                                     out=wexp4[:, u % 4,
                                               ci * 128:(ci + 1) * 128],
                                     in0=wexp2[:, par_u,
                                               ci * 128:(ci + 1) * 128],
                                     in1=rden[:, par_u, :], op=ALU.mult),
                                 inc=[("DV", 1)] if ci == np_u - 1 else ())
                        norm_msd[u] = c_["DV"]
                    if 2 <= it < 2 + len(pieces):
                        pieces[it - 2]()
                conv_pe_prev = c_["PE"]
                trans_ms = c_["dsp"]
                mixf_ac = c_["AC"]
        p.barrier()
        bfree = {i: None for i in range(8)}

        dma(w1t_s[:], w1qt_d.rearrange("(ic p) o -> p ic o", p=128))
        dma(w2t_s[:], w2qt_d.rearrange("(hc p) o -> p hc o", p=128),
            inc="dgp", eng=G)
        w2t_ms = c_["dgp"]
        dma(wo_s[:], wo_d.rearrange("(cc p) o -> p cc o", p=128),
            inc="dgp", eng=G)
        mxr = mix_d[sq].rearrange("(cc p) n -> p cc n", p=128)
        xqsrc = xq_d[sq].rearrange("(n p) c -> p n c", p=128)
        ffn1_ms = {0: 0, 1: 0}
        wo_ms = {0: 0, 1: 0}
        ld_ms = {0: 0, 1: 0}
        hq_ms = {}
        hq_dm = {}
        hqp_ms = {0: 0, 1: 0}
        quant_ms = {}
        sihc = {0: 12, 1: 13}
        ffn2_ms = 0
        silu_ms = 0
        xt_free = 0
        st_store = 0
        dmat(xqf_s[:, 0, :, :], xqsrc[:, 0, :])
        dma(mxs_s[:, 0, :, :], mxr[:, :, 0:128])
        ld_ms[0] = c_["dsp"]
        for i in range(NT + 1):
            t, u = i, i - 1
            tp_ = t % 2
            up = u % 2
            if t + 1 < NT:
                tn = (t + 1) % 2
                dmat(xqf_s[:, tn, :, :], xqsrc[:, t + 1, :],
                     wait=[("PE", ffn1_ms[tn])])
            if t < NT:
                scol = col0 + t
                gel_ms = []
                for hc8 in range(8):
                    bk = hc8 % 2
                    for ic in range(8):
                        p.op(T, lambda h, hc8=hc8, ic=ic, bk=bk, tp_=tp_: h.matmul(
                            bank(bk), xqf_s[:, tp_, ic, :],
                            w1t_s[:, ic, hc8 * 512:(hc8 + 1) * 512],
                            start=(ic == 0), stop=(ic == 7)),
                            wait=([("dsp", ld_ms[tp_])] + bwait(bk)
                                  if ic == 0 else ()),
                            inc=[("PE", 1)] if ic == 7 else ())
                    p.op(A, lambda h, hc8=hc8, bk=bk, scol=scol: h.activation(
                        out=g1_s[:, hc8 * 512:(hc8 + 1) * 512], in_=bank(bk),
                        func=AF.Gelu, scale=sxr2[:, scol:scol + 1]),
                        wait=[("PE", None), ("DV", quant_ms.get(t - 1, 0))],
                        inc=[("AC", 1)])
                    gel_ms.append(c_["AC"])
                    bfree[bk] = ("AC", c_["AC"])
                ffn1_ms[tp_] = c_["PE"]
            if u >= 0:
                ucol = col0 + u
                for hc in range(32):
                    for half in range(2):
                        p.op(T, lambda h, hc=hc, half=half: h.matmul(
                            bank(2 + half), hqhm[:, hc, :],
                            w2t_s[:, hc, half * 512:(half + 1) * 512],
                            start=(hc == 0), stop=(hc == 31)),
                            wait=(([("dsp", hq_dm[u][hc // 4])]
                                   if half == 0 else [])
                                  + ([("dgp", w2t_ms)] + bwait(2, 3)
                                     if hc == 0 and half == 0 else [])),
                            inc=[("PE", 1)] if hc == 31 and half == 1 else ())
                ffn2_ms = c_["PE"]
                for ccc in range(8):
                    for half in range(2):
                        p.op(T, lambda h, ccc=ccc, half=half, up=up: h.matmul(
                            bank(4 + half), mxs_s[:, up, ccc, :],
                            wo_s[:, ccc, half * 512:(half + 1) * 512],
                            start=(ccc == 0), stop=(ccc == 7)),
                            wait=([("dsp", ld_ms[up]), ("dgp", None)]
                                  + bwait(4, 5)
                                  if ccc == 0 and half == 0 else ()),
                            inc=[("PE", 1)] if ccc == 7 and half == 1 else ())
                wo_ms[up] = c_["PE"]
            if t + 1 < NT:
                tn = (t + 1) % 2
                dma(mxs_s[:, tn, :, :], mxr[:, :, (t + 1) * 128:(t + 2) * 128],
                    wait=[("PE", wo_ms[tn])])
                ld_ms[tn] = c_["dsp"]
            if t < NT:
                for hc8 in range(8):
                    p.op(V, lambda h, hc8=hc8: h.tensor_reduce(
                        out=red32[:, 24 + hc8:25 + hc8],
                        in_=g1_s[:, hc8 * 512:(hc8 + 1) * 512], axis=AX,
                        op=ALU.max, apply_absolute_value=True),
                        wait=[("AC", gel_ms[hc8])], ns=True)
                p.op(V, lambda h: h.tensor_reduce(
                    out=tmpc[:, 14:15], in_=red32[:, 24:32], axis=AX,
                    op=ALU.max), ns=True)
                p.op(V, lambda h: h.tensor_scalar(
                    out=tmpc[:, 15:16], in0=tmpc[:, 14:15], scalar1=1e-5,
                    scalar2=1.0 / 127.0, op0=ALU.max, op1=ALU.mult), ns=True)
                p.op(V, lambda h: h.reciprocal(
                    out=tmpc[:, 3:4], in_=tmpc[:, 15:16]), ns=True)
                p.op(V, lambda h, tp_=tp_: h.tensor_scalar(
                    out=tmpc[:, sihc[tp_]:sihc[tp_] + 1], in0=tmpc[:, 15:16],
                    scalar1=s1c[:, 1:2], scalar2=None, op0=ALU.mult), ns=True)
                hq_dm[t] = []
                for ei in range(8):
                    sl = slice(ei * 512, (ei + 1) * 512)
                    eb = ei % 2
                    p.op(V, lambda h, sl=sl: h.tensor_scalar(
                        out=qscr[:, 0:512], in0=g1_s[:, sl], scalar1=tmpc[:, 3:4],
                        scalar2=None, op0=ALU.mult))
                    p.op(V, lambda h: h.tensor_copy(
                        out=qscr[:, 512:1024].bitcast(I32), in_=qscr[:, 0:512]))
                    p.op(V, lambda h, eb=eb: h.tensor_copy(
                        out=hq_q[:, eb, :], in_=qscr[:, 512:1024].bitcast(I32)),
                        wait=([("dsp", hqp_ms[eb])] if hqp_ms[eb] else ()),
                        inc=[("DV", 1)])
                    dmat(hqhm[:, ei * 4:(ei + 1) * 4, :], hq_q[:, eb, :],
                         wait=[("DV", None)] + ([("PE", ffn2_ms)]
                                                if ei == 0 and u >= 0 else []))
                    hqp_ms[eb] = c_["dsp"]
                    hq_dm[t].append(c_["dsp"])
                quant_ms[t] = c_["DV"]
                hq_ms[t] = c_["dsp"]
            if u >= 0:
                ucol = col0 + u
                p.op(A, lambda h, up=up: h.activation(
                    out=gt_s[:, up, 0:512], in_=bank(2), func=AF.Silu,
                    scale=tmpc[:, sihc[up]:sihc[up] + 1]),
                    wait=[("PE", ffn2_ms), ("DV", xt_free)])
                p.op(A, lambda h, up=up: h.activation(
                    out=gt_s[:, up, 512:1024], in_=bank(3), func=AF.Silu,
                    scale=tmpc[:, sihc[up]:sihc[up] + 1]), inc=[("AC", 1)])
                silu_ms = c_["AC"]
                bfree[2] = ("AC", silu_ms)
                bfree[3] = ("AC", silu_ms)
                dma(xt_s[:], xr[:, u, :], inc="dgp", eng=G,
                    wait=[("DV", xt_free)])
                p.op(V, lambda h, ucol=ucol, up=up: h.tensor_scalar(
                    out=gt_s[:, 1 - up, :], in0=xt_s[:],
                    scalar1=rstd2[:, ucol:ucol + 1], scalar2=nmr2[:, ucol:ucol + 1],
                    op0=ALU.mult, op1=ALU.add), wait=[("dgp", None)])
                for half in range(2):
                    p.op(V, lambda h, half=half, up=up: h.scalar_tensor_tensor(
                        out=st_s[:, half * 512:(half + 1) * 512],
                        in0=bank(4 + half), scalar=1.0,
                        in1=gt_s[:, 1 - up, half * 512:(half + 1) * 512],
                        op0=ALU.mult, op1=ALU.add),
                        wait=([("PE", wo_ms[up]), ("dso", st_store)]
                              if half == 0 else ()))
                bfree[4] = ("DV", c_["DV"] + 1)
                bfree[5] = ("DV", c_["DV"] + 1)
                p.op(V, lambda h, up=up: h.tensor_tensor(
                    out=gt_s[:, 1 - up, :], in0=st_s[:], in1=gt_s[:, up, :],
                    op=ALU.mult), wait=[("AC", silu_ms)])
                p.op(V, lambda h, up=up: h.tensor_tensor(
                    out=st_s[:], in0=gt_s[:, 1 - up, :], in1=xt_s[:],
                    op=ALU.add), inc=[("DV", 1)])
                xt_free = c_["DV"]
                dma(out_d[sq].rearrange("(n p) c -> p n c", p=128)[:, u, :],
                    st_s[:], inc="dso", eng=G, wait=[("DV", None)])
                st_store = c_["dso"]
        p.barrier()
        bfree = {i: None for i in range(8)}

    if DEBUG:
        dbg_d = nc.dram_tensor("dbg_s", [128, 3, 64], F32, kind="ExternalOutput").ap()
        dma(dbg_d[:, 0, :], rstd2[:], inc="dst")
        dma(dbg_d[:, 1, :], nmr2[:], inc="dst")
        dma(dbg_d[:, 2, :], sxr2[:], inc="dst")

    p.op(SY, wait=[("dst", None), ("dso", None)])

    from contextlib import ExitStack
    with ExitStack() as stk:
        block = stk.enter_context(nc.Block())
        block.sync(lambda h: [f(h) for f in p.q[SY]])
        block.tensor(lambda h: [f(h) for f in p.q[T]])
        block.vector(lambda h: [f(h) for f in p.q[V]])
        block.scalar(lambda h: [f(h) for f in p.q[A]])
        block.gpsimd(lambda h: [f(h) for f in p.q[G]])
    return nc


_NC_CACHE = None


def kernel(L0_attn_x, L0_attn_Wqkv, L0_attn_Wo, L0_attn_conv_w, L0_ffn_w1, L0_ffn_w2):
    global _NC_CACHE
    x = np.ascontiguousarray(np.asarray(L0_attn_x, dtype=np.float32))
    wqkv = np.ascontiguousarray(np.asarray(L0_attn_Wqkv, dtype=np.float32))
    wo = np.ascontiguousarray(np.asarray(L0_attn_Wo, dtype=np.float32))
    convw = np.ascontiguousarray(np.asarray(L0_attn_conv_w, dtype=np.float32))
    w1 = np.ascontiguousarray(np.asarray(L0_ffn_w1, dtype=np.float32))
    w2 = np.ascontiguousarray(np.asarray(L0_ffn_w2, dtype=np.float32))

    if _NC_CACHE is None:
        _NC_CACHE = build_nc()
    nc = _NC_CACHE
    in_maps = [{
        "x": x[core * B2:(core + 1) * B2],
        "wqkv": wqkv, "wo": wo, "convw": convw, "w1": w1, "w2": w2,
    } for core in range(8)]
    res = run_bass_kernel_spmd(nc, in_maps, core_ids=list(range(8)))
    out = np.concatenate([r["out"] for r in res.results], axis=0)
    return out.astype(np.float32)


# version 17
# speedup vs baseline: 1.0377x; 1.0377x over previous
import os
import sys

for _p in ("/opt/trn_rl_repo",):
    if os.path.isdir(_p) and _p not in sys.path:
        sys.path.insert(0, _p)

import numpy as np
import ml_dtypes

import concourse.bass as bass
import concourse.mybir as mybir
from concourse.bass_utils import run_bass_kernel_spmd

F32 = mybir.dt.float32
BF16 = mybir.dt.bfloat16
I32 = mybir.dt.int32
AF = mybir.ActivationFunctionType
ALU = mybir.AluOpType
AX = mybir.AxisListType.X

B2 = 2
S = 4096
C = 1024
H = 8
D = 128
HID = 4096
NT = 32
CSET = [0, 1, 2, 4, 8, 16]
OFFS = [0] + [2 ** k for k in range(12)]
ISQ = 0.08838834764831845
EXPB = -30.0
MAGIC = 128.5

DEBUG = False
ENGINES = ("sync", "tensor", "vector", "scalar", "gpsimd")
SY, T, V, A, G = ENGINES


def npairs(t):
    return sum(1 for c in CSET if t - c >= 0)


class Prog:

    def __init__(self, nc):
        self.nc = nc
        self.q = {e: [] for e in ENGINES}
        self.sems = {}
        self.cnt = {}
        self.last_wait = {}

    def sem(self, *names):
        for n in names:
            self.sems[n] = self.nc.alloc_semaphore(n)
            self.cnt[n] = 0

    def op(self, eng, fn=None, wait=(), inc=(), ns=False):
        if eng == "vector":
            wait = list(wait) + [("VS", self.cnt["VS"])]
            if ns:
                inc = list(inc) + [("VS", 1)]
        wl = []
        for s, v in wait:
            v = self.cnt[s] if v is None else v
            k = (eng, s)
            if v <= 0 or self.last_wait.get(k, -1) >= v:
                continue
            self.last_wait[k] = v
            wl.append((s, v))
        il = []
        for s, n in inc:
            self.cnt[s] += n
            il.append((s, n))
        sems = self.sems

        def run(h, wl=wl, il=il, fn=fn):
            for s, v in wl:
                h.wait_ge(sems[s], v)
            ins = fn(h) if fn is not None else None
            if il:
                if ins is None:
                    ins = h.nop()
                for s, n in il:
                    ins = ins.then_inc(sems[s], n)
        self.q[eng].append(run)

    def barrier(self):
        for e in ENGINES:
            if e == SY:
                self.op(e, lambda h: h.nop(),
                        wait=[(s, None) for s in ("dsp", "dst", "dgp", "dso")],
                        inc=[("stg", 1)])
            else:
                self.op(e, lambda h: h.drain(), inc=[("stg", 1)])
        cur = self.cnt["stg"]
        for e in ENGINES:
            self.op(e, wait=[("stg", cur)])


RACEDET = False


def build_nc():
    nc = bass.Bass("TRN2", target_bir_lowering=False,
                   detect_race_conditions=RACEDET)

    x_d = nc.dram_tensor("x", [B2, S, C], F32, kind="ExternalInput").ap()
    wqkv_d = nc.dram_tensor("wqkv", [C, 3 * C], F32, kind="ExternalInput").ap()
    wo_d = nc.dram_tensor("wo", [C, C], F32, kind="ExternalInput").ap()
    convw_d = nc.dram_tensor("convw", [C, 7], F32, kind="ExternalInput").ap()
    w1_d = nc.dram_tensor("w1", [HID, C], F32, kind="ExternalInput").ap()
    w2_d = nc.dram_tensor("w2", [C, HID], F32, kind="ExternalInput").ap()
    out_d = nc.dram_tensor("out", [B2, S, C], F32, kind="ExternalOutput").ap()

    _k = "ExternalOutput" if DEBUG else "Internal"
    w1qt_d = nc.dram_tensor("w1qt_s", [C, HID], BF16, kind=_k).ap()
    w2qt_d = nc.dram_tensor("w2qt_s", [HID, C], BF16, kind=_k).ap()
    xq_d = nc.dram_tensor("xq_s", [B2, S, C], BF16, kind=_k).ap()
    mix_d = nc.dram_tensor("mix_s", [B2, C, S], BF16, kind=_k).ap()

    wl_ = np.arange(128)[:, None]
    ql_ = np.arange(128)[None, :]
    masks_np = np.stack(
        [np.isin(128 * c + ql_ - wl_, OFFS).astype(np.float32) for c in CSET])
    masks_i = nc.inline_tensor(
        masks_np.transpose(1, 0, 2).astype(ml_dtypes.bfloat16).copy(), "masks_c")
    ident_i = nc.inline_tensor(np.eye(128, dtype=ml_dtypes.bfloat16), "ident_c")

    BASE = 16512

    def alloc(name, shape, dtype, offset=0):
        assert offset + BASE + int(np.prod(shape[1:])) * mybir.dt.size(dtype)             <= nc.sbuf_top, (name, offset)
        return nc.alloc_sbuf_tensor_at(name, list(shape), dtype, offset=offset + BASE)
    off = [0]

    def sal(name, shape, dtype, at=None):
        nbytes = int(np.prod(shape[1:])) * mybir.dt.size(dtype)
        if at is None:
            at = off[0]
        t = alloc(name, list(shape), dtype, offset=at)
        off[0] = max(off[0], at + ((nbytes + 31) // 32) * 32)
        return t

    masks_s = sal("masks_s", [128, 6, 128], BF16)
    ones_b = sal("ones_b", [128, 128], BF16)
    ones_f = sal("ones_f", [128, 128], F32)
    ident_s = sal("ident_s", [128, 128], BF16)
    cwf_s = sal("cwf_s", [128, H, 7], F32)
    diag_s = sal("diag_s", [128, 7, 128], BF16)
    cc = sal("cc", [128, 16], F32)
    tmpc = sal("tmpc", [128, 16], F32)
    red32 = sal("red32", [128, 32], F32)
    rstd2 = sal("rstd2", [128, 64], F32)
    nmr2 = sal("nmr2", [128, 64], F32)
    sxr2 = sal("sxr2", [128, 64], F32)
    s1c = sal("s1c", [128, 2], F32)
    assert off[0] <= 5664, off[0]
    K8 = 5664

    wfb = alloc("wfb", [128, 33, 1024], F32, offset=K8)
    wqb_s = alloc("wqb_s", [128, 32, 1024], BF16, offset=K8 + 135168)
    tstg = alloc("tstg", [128, 2, 8, 128], BF16, offset=K8 + 135168 + 65536)
    c2q = alloc("c2q", [128, 1024], BF16, offset=K8 + 135168 + 65536 + 4096)
    assert K8 + 135168 + 65536 + 4096 + 2048 <= 212832

    xn_fm = alloc("xn_fm", [128, NT, 8, 128], BF16, offset=K8)
    AT = K8 + 65536
    q_s = alloc("q_s", [128, S], BF16, offset=AT)
    k_s = alloc("k_s", [128, S], BF16, offset=AT + 8192)
    vpad = alloc("vpad", [128, 64 + S], BF16, offset=AT + 16384)
    vsT = alloc("vsT", [128, NT, 128], BF16, offset=AT + 24704)
    fq_s = alloc("fq_s", [128, S], BF16, offset=AT + 32896)
    fk_s = alloc("fk_s", [128, S], BF16, offset=AT + 41088)
    fksT = alloc("fksT", [128, NT, 128], BF16, offset=AT + 49280)
    tmpz = alloc("tmpz", [128, S], BF16, offset=AT + 57472)
    mixf = alloc("mixf", [128, S], F32, offset=AT + 65664)
    rz_s = alloc("rz_s", [128, S], F32, offset=AT + 82048)
    wexp = alloc("wexp", [128, 2, 768], BF16, offset=AT + 98432)
    rden = alloc("rden", [128, 2, 128], F32, offset=AT + 101504)
    wqkv_s = alloc("wqkv_s", [128, 2, 3, 8, 128], BF16, offset=AT + 102528)
    kv_b = alloc("kv_b", [128, 128], BF16, offset=AT + 114816)
    mixb = alloc("mixb", [128, S], BF16, offset=AT + 115104)
    wexp2 = alloc("wexp2", [128, 2, 768], BF16, offset=AT + 123296)
    zscr = alloc("zscr", [128, 512], F32, offset=AT + 126368)
    wexp4 = alloc("wexp4", [128, 4, 768], BF16, offset=AT + 128416)
    assert AT + 134560 <= 212832
    x4_s = alloc("x4_s", [128, 2, 4, 1024], F32, offset=AT)
    sqs_s = alloc("sqs_s", [128, 1024], F32, offset=AT + 32896)
    xnf_s = alloc("xnf_s", [128, 4, 1024], F32, offset=AT + 36992)
    xnb_s = alloc("xnb_s", [128, 2, 1024], BF16, offset=AT + 53376)
    xqi_s = alloc("xqi_s", [128, 1024], F32, offset=AT + 57472)
    xqb_s = alloc("xqb_s", [128, 2, 1024], BF16, offset=AT + 61568)
    assert AT + 65664 <= 212832

    w1t_s = alloc("w1t_s", [128, 8, HID], BF16, offset=K8)
    w2t_s = alloc("w2t_s", [128, 32, 1024], BF16, offset=AT)
    wo_s = alloc("wo_s", [128, 8, 1024], BF16, offset=AT + 65664)
    g1_s = alloc("g1_s", [128, HID], F32, offset=AT + 82048)
    qscr = alloc("qscr", [128, 2048], F32, offset=AT + 98432)
    hq_q = alloc("hq_q", [128, 2, 512], BF16, offset=AT + 102528)
    hqhm = alloc("hqhm", [128, 32, 128], BF16, offset=AT + 106624)
    P3L = AT + 114816
    xqf_s = alloc("xqf_s", [128, 2, 8, 128], BF16, offset=P3L)
    mxs_s = alloc("mxs_s", [128, 2, 8, 128], BF16, offset=P3L + 4096)
    xt_s = alloc("xt_s", [128, 1024], F32, offset=P3L + 8192)
    st_s = alloc("st_s", [128, 1024], F32, offset=P3L + 12288)
    gt_s = alloc("gt_s", [128, 2, 1024], F32, offset=P3L + 16384)
    assert P3L + 24576 <= 212832, P3L + 24576

    ps = nc.alloc_psum_tensor("ps", [128, 8, 512], F32)
    psf = ps.rearrange("p b n -> p (b n)")

    def bank(i, a=0, n=512):
        return ps[:, i, a:a + n]

    p = Prog(nc)
    p.sem("stg", "dsp", "dgp", "dst", "dso", "PE", "DV", "AC", "VS")
    c_ = p.cnt

    bfree = {i: None for i in range(8)}

    def bwait(*bks):
        return [bfree[b] for b in set(bks) if bfree[b] is not None]

    def dma(dst, src, inc="dsp", eng=SY, wait=()):
        p.op(eng, lambda h, d=dst, s=src: h.dma_start(out=d, in_=s),
             wait=wait, inc=[(inc, 16)])

    def dmat(dst, src, wait=()):
        p.op(SY, lambda h, d=dst, s=src: h.dma_start(out=d, in_=s, transpose=True),
             wait=wait, inc=[("dsp", 16)])

    def emit_floor(u, r1i, r1f, cbuf, out_ap, wait_last=(), inc_last=()):
        p.op(V, lambda h: h.tensor_copy(out=r1i, in_=u))
        p.op(V, lambda h: h.tensor_copy(out=r1f, in_=r1i))
        p.op(V, lambda h: h.tensor_tensor(out=cbuf, in0=r1f, in1=u, op=ALU.is_gt))
        p.op(V, lambda h: h.scalar_tensor_tensor(
            out=out_ap, in0=r1f, scalar=-128.0, in1=cbuf,
            op0=ALU.add, op1=ALU.subtract), wait=wait_last, inc=inc_last)

    dma(masks_s[:], masks_i.ap())
    dma(ident_s[:], ident_i.ap())
    dma(cwf_s[:], convw_d.rearrange("(h p) t -> p h t", p=128))
    for dst, val in ((ones_b[:], 1.0), (ones_f[:], 1.0), (cc[:, 0:1], EXPB),
                     (cc[:, 1:2], ISQ), (cc[:, 2:3], 1e-5), (cc[:, 3:4], 1e-6)):
        p.op(V, lambda h, d=dst, v=val: h.memset(d, v), inc=[("DV", 1)])

    prev_dv = None
    for wsrc, wtd, nrc, ncol, sidx in (
            (w1_d, w1qt_d, 32, 1024, 0), (w2_d, w2qt_d, 8, HID, 1)):
        if ncol == 1024:
            src_r = wsrc.rearrange("(rc p) i -> p rc i", p=128)
            for i in range(4):
                dma(wfb[:, i * 8:(i + 1) * 8, :], src_r[:, i * 8:(i + 1) * 8, :],
                    inc="dsp" if i % 2 == 0 else "dgp",
                    eng=SY if i % 2 == 0 else G)
        else:
            src_r = wsrc.rearrange("(rc p) (q i) -> p rc q i", p=128, i=1024)
            for i in range(4):
                dma(wfb[:, 0:32, :].rearrange("p (rc q) i -> p rc q i", q=4)
                    [:, i * 2:(i + 1) * 2, :, :],
                    src_r[:, i * 2:(i + 1) * 2, :, :],
                    inc="dsp" if i % 2 == 0 else "dgp",
                    eng=SY if i % 2 == 0 else G,
                    wait=[("DV", prev_dv[8 * (i + 1) - 1])])
        p.op(V, lambda h: h.tensor_reduce(
            out=red32[:, 0:32], in_=wfb[:, 0:32, :], axis=AX, op=ALU.add,
            apply_absolute_value=True),
            wait=[("dsp", None), ("dgp", None)], ns=True)
        p.op(V, lambda h: h.tensor_reduce(
            out=tmpc[:, 0:1], in_=red32[:, 0:32], axis=AX, op=ALU.add),
            inc=[("DV", 1)])
        p.op(T, lambda h: h.matmul(bank(0, 0, 1), ones_f[:], tmpc[:, 0:1],
                                   start=True, stop=True),
             wait=[("DV", None)] + bwait(0), inc=[("PE", 1)])
        sc = s1c[:, sidx:sidx + 1]
        p.op(V, lambda h, sc=sc: h.tensor_scalar(
            out=sc, in0=bank(0, 0, 1), scalar1=1.0 / (32 * 128 * 1024),
            scalar2=1e-5, op0=ALU.mult, op1=ALU.max), wait=[("PE", None)], ns=True)
        bfree[0] = ("DV", c_["DV"] + 1)
        p.op(V, lambda h, sc=sc: h.reciprocal(out=tmpc[:, 1:2], in_=sc),
             ns=True)
        st_ms = {0: 0, 1: 0}
        wtr = wtd.rearrange("(ic p) o -> p ic o", p=128)
        scr = wfb[:, 32, :]
        prev_dv = []
        for b in range(32):
            wqv = wqb_s[:, b, :]
            p.op(V, lambda h, b=b: h.tensor_scalar(
                out=scr, in0=wfb[:, b, :], scalar1=tmpc[:, 1:2],
                scalar2=None, op0=ALU.mult))
            p.op(V, lambda h: h.tensor_scalar(
                out=c2q[:], in0=scr, scalar1=-0.5,
                scalar2=None, op0=ALU.is_lt))
            p.op(V, lambda h, b=b: h.scalar_tensor_tensor(
                out=wqb_s[:, b, :], in0=scr, scalar=0.5, in1=c2q[:],
                op0=ALU.is_gt, op1=ALU.subtract), inc=[("DV", 1)])
            prev_dv.append(c_["DV"])
            rc, qq = (b, 0) if ncol == 1024 else (b // 4, b % 4)
            par = b % 2
            tv = tstg[:, par, :, :]
            dmat(tv, wqv, wait=[("DV", None), ("dst", st_ms[par])])
            dst = wtr[:, qq * 8:(qq + 1) * 8, rc * 128:(rc + 1) * 128]
            dma(dst, tv, inc="dst", wait=[("dsp", None)])
            st_ms[par] = c_["dst"]

    p.barrier()
    bfree = {i: None for i in range(8)}

    w_ms = {0: 0, 1: 0}
    for sq in range(B2):
        col0 = sq * 32
        xr = x_d[sq].rearrange("(n p) c -> p n c", p=128)
        xqr = xq_d[sq].rearrange("(n p) c -> p n c", p=128)

        grp_ms = {0: (0, 0), 1: (0, 0)}
        xqst_ms = {0: 0, 1: 0}
        xnbdm_ms = {0: 0, 1: 0}
        sqs_dv = 0
        for grp in range(8):
            gp = grp % 2
            dma(x4_s[:, gp, :, :], xr[:, grp * 4:grp * 4 + 4, :],
                inc="dgp", eng=G,
                wait=[("DV", grp_ms[gp][0]), ("AC", grp_ms[gp][1])])
            scol0 = col0 + grp * 4
            p.op(V, lambda h, gp=gp: h.tensor_reduce(
                out=red32[:, 0:4], in_=x4_s[:, gp, :, :], axis=AX, op=ALU.add),
                wait=[("dgp", None)], ns=True)
            for j in range(4):
                p.op(A, lambda h, gp=gp, j=j: h.activation(
                    out=sqs_s[:], in_=x4_s[:, gp, j, :], func=AF.Square,
                    accum_out=red32[:, 4 + j:5 + j]),
                    wait=[("dgp", None), ("DV", sqs_dv)], inc=[("AC", 1)])
            p.op(V, lambda h: h.tensor_scalar(
                out=red32[:, 8:12], in0=red32[:, 0:4], scalar1=1.0 / C,
                scalar2=None, op0=ALU.mult), wait=[("AC", None)], ns=True)
            p.op(V, lambda h: h.tensor_scalar(
                out=red32[:, 12:16], in0=red32[:, 4:8], scalar1=1.0 / C,
                scalar2=None, op0=ALU.mult), ns=True)
            p.op(V, lambda h: h.tensor_tensor(
                out=red32[:, 16:20], in0=red32[:, 8:12], in1=red32[:, 8:12],
                op=ALU.mult), ns=True)
            p.op(V, lambda h: h.tensor_tensor(
                out=red32[:, 20:24], in0=red32[:, 12:16], in1=red32[:, 16:20],
                op=ALU.subtract), inc=[("DV", 1)])
            p.op(A, lambda h: h.activation(
                out=red32[:, 24:28], in_=red32[:, 20:24], func=AF.Sqrt,
                bias=cc[:, 2:3], scale=1.0),
                wait=[("DV", None)], inc=[("AC", 1)])
            p.op(V, lambda h, scol0=scol0: h.reciprocal(
                out=rstd2[:, scol0:scol0 + 4], in_=red32[:, 24:28]),
                wait=[("AC", None)], ns=True)
            p.op(V, lambda h, scol0=scol0: h.tensor_tensor(
                out=red32[:, 28:32], in0=red32[:, 8:12],
                in1=rstd2[:, scol0:scol0 + 4], op=ALU.mult), ns=True)
            p.op(V, lambda h, scol0=scol0: h.tensor_scalar(
                out=nmr2[:, scol0:scol0 + 4], in0=red32[:, 28:32], scalar1=-1.0,
                scalar2=None, op0=ALU.mult), ns=True)
            for j in range(4):
                t = grp * 4 + j
                xt = x4_s[:, gp, j, :]
                scol = col0 + t
                p.op(V, lambda h, xt=xt, scol=scol, j=j: h.tensor_scalar(
                    out=xnf_s[:, j, :], in0=xt,
                    scalar1=rstd2[:, scol:scol + 1], scalar2=nmr2[:, scol:scol + 1],
                    op0=ALU.mult, op1=ALU.add), inc=[("DV", 1)])
                p.op(A, lambda h, xt=xt, j=j, scol=scol: h.activation(
                    out=xnb_s[:, j % 2, :], in_=xt, func=AF.Identity,
                    scale=rstd2[:, scol:scol + 1], bias=nmr2[:, scol:scol + 1]),
                    wait=[("dsp", xnbdm_ms[j % 2]), ("DV", None)],
                    inc=[("AC", 1)])
                dmat(xn_fm[:, t, :, :], xnb_s[:, j % 2, :], wait=[("AC", None)])
                xnbdm_ms[j % 2] = c_["dsp"]
                p.op(V, lambda h, j=j: h.tensor_reduce(
                    out=red32[:, 8 + j:9 + j], in_=xnf_s[:, j, :], axis=AX,
                    op=ALU.max, apply_absolute_value=True), ns=True)
            p.op(V, lambda h: h.tensor_scalar(
                out=red32[:, 12:16], in0=red32[:, 8:12], scalar1=1e-5,
                scalar2=1.0 / 127.0, op0=ALU.max, op1=ALU.mult), ns=True)
            p.op(V, lambda h: h.reciprocal(
                out=red32[:, 16:20], in_=red32[:, 12:16]), ns=True)
            p.op(V, lambda h, scol0=scol0: h.tensor_scalar(
                out=sxr2[:, scol0:scol0 + 4], in0=red32[:, 12:16],
                scalar1=s1c[:, 0:1], scalar2=None, op0=ALU.mult), ns=True)
            for j in range(4):
                t = grp * 4 + j
                p.op(V, lambda h, j=j: h.tensor_scalar(
                    out=xqi_s[:], in0=xnf_s[:, j, :],
                    scalar1=red32[:, 16 + j:17 + j],
                    scalar2=None, op0=ALU.mult))
                p.op(V, lambda h: h.tensor_copy(
                    out=sqs_s[:].bitcast(I32), in_=xqi_s[:]))
                p.op(V, lambda h, j=j: h.tensor_copy(
                    out=xqb_s[:, j % 2, :], in_=sqs_s[:].bitcast(I32)),
                    wait=[("dst", xqst_ms[j % 2])], inc=[("DV", 1)])
                dma(xqr[:, t, :], xqb_s[:, j % 2, :], inc="dst",
                    wait=[("DV", None)])
                xqst_ms[j % 2] = c_["dst"]
            sqs_dv = c_["DV"]
            grp_ms[gp] = (c_["DV"], c_["AC"])
        p.barrier()
        bfree = {i: None for i in range(8)}

        p.op(V, lambda h: h.memset(vpad[:, 58:64], 0.0), inc=[("DV", 1)])
        conv_pe_prev = 0
        kv_pe = 0
        low_pe = 0
        mixb_ms = 0
        mixst_ms = 0
        trans_ms = 0
        vsT_ms = 0
        qk_ms = 0
        mixf_ac = 0
        tmz_ms = 0
        eqms = {}
        ekms = {}
        fqms = {}
        for hh in range(H + 1):
            hv = hh - 1
            if hh < H:
                hp = hh % 2
                for pi in range(3):
                    src = wqkv_d[:, pi * C + hh * D: pi * C + hh * D + D]
                    dma(wqkv_s[:, hp, pi, :, :],
                        src.rearrange("(ic p) o -> p ic o", p=128),
                        inc="dgp", eng=G, wait=[("PE", w_ms[hp])])
                for tap in range(7):
                    p.op(V, lambda h, tap=tap, hh=hh: h.tensor_scalar(
                        out=diag_s[:, tap, :], in0=ident_s[:],
                        scalar1=cwf_s[:, hh, tap:tap + 1], scalar2=None,
                        op0=ALU.mult),
                        wait=[("PE", conv_pe_prev),
                              ("dsp", 48 if hh == 0 and sq == 0 else 0)],
                        inc=[("DV", 1)] if tap == 6 else ())
            if hv >= 0:
                for ch in range(NT):
                    p.op(T, lambda h, ch=ch: h.matmul(
                        bank(0, 0, 128), fksT[:, ch, :], vsT[:, ch, :],
                        start=(ch == 0), stop=(ch == NT - 1)),
                        wait=([("dsp", trans_ms)] + bwait(0)
                              if ch == 0 else ()),
                        inc=[("PE", 1)] if ch == NT - 1 else ())
                kv_pe = c_["PE"]
                p.op(V, lambda h: h.tensor_copy(out=kv_b[:], in_=bank(0, 0, 128)),
                     wait=[("PE", kv_pe), ("PE", low_pe)], inc=[("DV", 1)])
                bfree[0] = ("DV", c_["DV"])
            if hv >= 0:
                p.op(V, lambda h: h.tensor_scalar(
                    out=tmpz[:], in0=fq_s[:], scalar1=tmpc[:, 13:14],
                    scalar2=None, op0=ALU.mult),
                    wait=[("PE", low_pe)], inc=[("DV", 1)])
                tmz_ms = c_["DV"]
                for tc in range(8):
                    bk = 2 + tc % 2
                    p.op(T, lambda h, tc=tc, bk=bk: h.matmul(
                        bank(bk), ones_b[:], tmpz[:, tc * 512:(tc + 1) * 512],
                        start=True, stop=True),
                        wait=[("DV", tmz_ms)] + bwait(bk) if tc < 2
                        else bwait(bk),
                        inc=[("PE", 1)])
                    p.op(V, lambda h, tc=tc, bk=bk: h.tensor_scalar(
                        out=zscr[:], in0=bank(bk),
                        scalar1=1e-6, scalar2=None, op0=ALU.add),
                        wait=[("PE", None)])
                    p.op(V, lambda h, tc=tc: h.reciprocal(
                        out=rz_s[:, tc * 512:(tc + 1) * 512],
                        in_=zscr[:]), inc=[("DV", 1)])
                    bfree[bk] = ("DV", c_["DV"])
                p.op(V, lambda h: h.tensor_tensor(
                    out=tmpz[:], in0=fq_s[:], in1=rz_s[:], op=ALU.mult),
                    wait=[("PE", c_["PE"])], inc=[("DV", 1)])
                tz2_ms = c_["DV"]
            if hh < H:
                for pi, dstb in ((0, q_s), (1, k_s), (2, vpad)):
                    for tc in range(8):
                        bk = tc % 2
                        for kc in range(8):
                            p.op(T, lambda h, pi=pi, tc=tc, kc=kc, bk=bk,
                                 hp=hp: h.matmul(
                                     bank(bk), wqkv_s[:, hp, pi, kc, :],
                                     xn_fm[:, 4 * tc:4 * tc + 4, kc, :],
                                     start=(kc == 0), stop=(kc == 7)),
                                 wait=([("dgp", None)] + bwait(bk)
                                       if kc == 0 else ()),
                                 inc=[("PE", 1)] if kc == 7 else ())
                        o0 = 64 + tc * 512 if pi == 2 else tc * 512
                        p.op(A, lambda h, dstb=dstb, o0=o0, bk=bk: h.activation(
                            out=dstb[:, o0:o0 + 512], in_=bank(bk),
                            func=AF.Copy),
                            wait=[("PE", None)], inc=[("AC", 1)])
                        bfree[bk] = ("AC", c_["AC"])
                w_ms[hp] = c_["PE"]
                dmat(vsT[:], vpad[:, 64:64 + S],
                     wait=[("AC", None), ("PE", kv_pe)])
                vsT_ms = c_["dsp"]
                qk_ms = c_["AC"]
            if hv >= 0:
                for tc in range(8):
                    bk = 4 + tc % 2
                    p.op(T, lambda h, tc=tc, bk=bk: h.matmul(
                        bank(bk), kv_b[:], tmpz[:, tc * 512:(tc + 1) * 512],
                        start=True, stop=True),
                        wait=[("DV", tz2_ms)] + bwait(bk) if tc < 2
                        else bwait(bk),
                        inc=[("PE", 1)])
                    p.op(V, lambda h, tc=tc, bk=bk: h.scalar_tensor_tensor(
                        out=mixb[:, tc * 512:(tc + 1) * 512], in0=bank(bk),
                        scalar=1.0, in1=mixf[:, tc * 512:(tc + 1) * 512],
                        op0=ALU.mult, op1=ALU.add),
                        wait=[("PE", None)]
                        + ([("dst", mixst_ms), ("AC", mixf_ac)]
                           if tc == 0 else []),
                        inc=[("DV", 1)])
                    bfree[bk] = ("DV", c_["DV"])
                low_pe = c_["PE"]
                mixb_ms = c_["DV"]
                dma(mix_d[sq, hv * D:(hv + 1) * D, :], mixb[:], inc="dst",
                    wait=[("DV", None)])
                mixst_ms = c_["dst"]
            if hh < H:
                score_ms = {}
                exp_ms = {}
                mask_msd = {}
                denom_ms = {}
                norm_msd = {}
                tele_ms = {}
                low_pe0 = low_pe
                kv_pe0 = kv_pe

                def pc_eq(ck):
                    sl = slice(ck * 1024, (ck + 1) * 1024)
                    p.op(A, lambda h, sl=sl: h.activation(
                        out=tmpz[:, sl], in_=q_s[:, sl], func=AF.Exp),
                        wait=[("PE", low_pe0)] if ck == 0 else (),
                        inc=[("AC", 1)])
                    eqms[ck] = c_["AC"]

                def pc_fq(ck):
                    sl = slice(ck * 1024, (ck + 1) * 1024)
                    p.op(V, lambda h, sl=sl: h.scalar_tensor_tensor(
                        out=fq_s[:, sl], in0=q_s[:, sl], scalar=1.0,
                        in1=tmpz[:, sl], op0=ALU.add, op1=ALU.min),
                        wait=[("AC", eqms[ck])], inc=[("DV", 1)])
                    fqms[ck] = c_["DV"]

                def pc_ek(ck):
                    sl = slice(ck * 1024, (ck + 1) * 1024)
                    p.op(A, lambda h, sl=sl: h.activation(
                        out=tmpz[:, sl], in_=k_s[:, sl], func=AF.Exp),
                        wait=[("DV", fqms[ck])], inc=[("AC", 1)])
                    ekms[ck] = c_["AC"]
                    p.op(V, lambda h, sl=sl: h.scalar_tensor_tensor(
                        out=fk_s[:, sl], in0=k_s[:, sl], scalar=1.0,
                        in1=tmpz[:, sl], op0=ALU.add, op1=ALU.min),
                        wait=[("AC", ekms[ck])], inc=[("DV", 1)])
                    dmat(fksT[:, ck * 8:(ck + 1) * 8, :], fk_s[:, sl],
                         wait=[("DV", None)]
                         + ([("PE", kv_pe0)] if ck == 0 else []))

                def pc_fksum():
                    for ck in range(4):
                        sl = slice(ck * 1024, (ck + 1) * 1024)
                        p.op(V, lambda h, ck=ck, sl=sl: h.tensor_reduce(
                            out=red32[:, 16 + ck:17 + ck], in_=fk_s[:, sl],
                            axis=AX, op=ALU.add), ns=True)
                    p.op(V, lambda h: h.tensor_reduce(
                        out=tmpc[:, 13:14], in_=red32[:, 16:20], axis=AX,
                        op=ALU.add), ns=True)

                pieces = ([lambda ck=ck: pc_eq(ck) for ck in range(4)]
                          + [lambda ck=ck: pc_fq(ck) for ck in range(4)]
                          + [lambda ck=ck: pc_ek(ck) for ck in range(4)]
                          + [pc_fksum])
                for it in range(NT + 2):
                    t, u, v = it, it - 1, it - 2
                    if t < NT:
                        par = t % 2
                        np_t = npairs(t)
                        sb = 2 + 2 * par
                        nsb = min(np_t, 4)
                        for ci in range(np_t):
                            w = t - CSET[ci]
                            bnk = sb if ci < 4 else sb + 1
                            a0 = (ci % 4) * 128
                            first = (ci == 0) or (ci == 4)
                            last = (ci == nsb - 1) or (ci == np_t - 1)
                            p.op(T, lambda h, w=w, t=t, bnk=bnk, a0=a0,
                                 f=first, l=last: h.matmul(
                                     bank(bnk, a0, 128),
                                     k_s[:, w * 128:(w + 1) * 128],
                                     q_s[:, t * 128:(t + 1) * 128],
                                     start=f, stop=l),
                                 wait=(([("AC", qk_ms)] if t < 2 else [])
                                       + bwait(sb, sb + 1)) if ci == 0 else (),
                                 inc=[("PE", 1)] if ci == np_t - 1 else ())
                        score_ms[t] = c_["PE"]
                    if 0 <= u < NT:
                        par_u = u % 2
                        np_u = npairs(u)
                        sbu = 2 + 2 * par_u
                        for ci in range(np_u):
                            p.op(T, lambda h, ci=ci, par_u=par_u, sbu=sbu,
                                 np_u=np_u: h.matmul(
                                     bank(sbu + 1, 256, 128), ones_b[:],
                                     wexp2[:, par_u, ci * 128:(ci + 1) * 128],
                                     start=(ci == 0), stop=(ci == np_u - 1)),
                                 wait=([("DV", mask_msd[u])] + bwait(sbu + 1)
                                       if ci == 0 else ()),
                                 inc=[("PE", 1)] if ci == np_u - 1 else ())
                        denom_ms[u] = c_["PE"]
                    if 0 <= v < NT:
                        par_v = v % 2
                        np_v = npairs(v)
                        dtv = 6 + par_v
                        for ci in range(np_v):
                            w = v - CSET[ci]
                            p.op(T, lambda h, ci=ci, w=w, v=v,
                                 dtv=dtv: h.matmul(
                                     bank(dtv, 0, 128), vsT[:, w, :],
                                     wexp4[:, v % 4, ci * 128:(ci + 1) * 128],
                                     start=(ci == 0), stop=False),
                                 wait=[("DV", norm_msd[v]), ("dsp", vsT_ms)]
                                 + bwait(dtv) if ci == 0 else ())
                        for tap in range(7):
                            p.op(T, lambda h, tap=tap, v=v, dtv=dtv: h.matmul(
                                bank(dtv, 0, 128), diag_s[:, tap, :],
                                vpad[:, 58 + tap + v * 128:
                                     58 + tap + v * 128 + 128],
                                start=False, stop=(tap == 6)),
                                inc=[("PE", 1)] if tap == 6 else ())
                        tele_ms[v] = c_["PE"]
                    if t < NT:
                        p.op(A, lambda h, np_t=np_t, sb=sb, t=t: h.activation(
                            out=wexp4[:, t % 4, 0:np_t * 128],
                            in_=psf[:, sb * 512: sb * 512 + np_t * 128],
                            func=AF.Exp, bias=cc[:, 0:1], scale=cc[:, 1:2]),
                            wait=[("PE", score_ms[t]),
                                  ("PE", tele_ms.get(t - 4, conv_pe_prev))],
                            inc=[("AC", 1)])
                        exp_ms[t] = c_["AC"]
                        bfree[sb] = ("AC", exp_ms[t])
                    if 0 <= v < NT:
                        p.op(A, lambda h, v=v, dtv=dtv: h.activation(
                            out=mixf[:, v * 128:(v + 1) * 128],
                            in_=bank(dtv, 0, 128), func=AF.Copy),
                            wait=[("PE", tele_ms[v])]
                            + ([("DV", mixb_ms)] if v == 0 else []),
                            inc=[("AC", 1)])
                        bfree[dtv] = ("AC", c_["AC"])
                    if t < NT:
                        p.op(V, lambda h, np_t=np_t, t=t: h.tensor_tensor(
                            out=wexp2[:, t % 2, 0:np_t * 128],
                            in0=wexp4[:, t % 4, 0:np_t * 128],
                            in1=masks_s[:, 0:np_t, :]
                            .rearrange("p a b -> p (a b)"),
                            op=ALU.mult),
                            wait=[("AC", exp_ms[t]),
                                  ("PE", denom_ms.get(t - 2, conv_pe_prev))],
                            inc=[("DV", 1)])
                        mask_msd[t] = c_["DV"]
                    if 0 <= u < NT:
                        p.op(V, lambda h, par_u=par_u, sbu=sbu: h.reciprocal(
                            out=rden[:, par_u, :],
                            in_=bank(sbu + 1, 256, 128)),
                            wait=[("PE", denom_ms[u])], inc=[("DV", 1)])
                        bfree[sbu + 1] = ("DV", c_["DV"])
                        for ci in range(np_u):
                            p.op(V, lambda h, ci=ci, u=u,
                                 par_u=par_u: h.tensor_tensor(
                                     out=wexp4[:, u % 4,
                                               ci * 128:(ci + 1) * 128],
                                     in0=wexp2[:, par_u,
                                               ci * 128:(ci + 1) * 128],
                                     in1=rden[:, par_u, :], op=ALU.mult),
                                 inc=[("DV", 1)] if ci == np_u - 1 else ())
                        norm_msd[u] = c_["DV"]
                    if 2 <= it < 2 + len(pieces):
                        pieces[it - 2]()
                conv_pe_prev = c_["PE"]
                trans_ms = c_["dsp"]
                mixf_ac = c_["AC"]
        p.barrier()
        bfree = {i: None for i in range(8)}

        dma(w1t_s[:], w1qt_d.rearrange("(ic p) o -> p ic o", p=128))
        dma(w2t_s[:], w2qt_d.rearrange("(hc p) o -> p hc o", p=128),
            inc="dgp", eng=G)
        w2t_ms = c_["dgp"]
        dma(wo_s[:], wo_d.rearrange("(cc p) o -> p cc o", p=128),
            inc="dgp", eng=G)
        mxr = mix_d[sq].rearrange("(cc p) n -> p cc n", p=128)
        xqsrc = xq_d[sq].rearrange("(n p) c -> p n c", p=128)
        ffn1_ms = {0: 0, 1: 0}
        wo_ms = {0: 0, 1: 0}
        ld_ms = {0: 0, 1: 0}
        hq_ms = {}
        hq_dm = {}
        hqp_ms = {0: 0, 1: 0}
        quant_ms = {}
        sihc = {0: 12, 1: 13}
        ffn2_ms = 0
        silu_ms = 0
        xt_free = 0
        st_store = 0
        dmat(xqf_s[:, 0, :, :], xqsrc[:, 0, :])
        dma(mxs_s[:, 0, :, :], mxr[:, :, 0:128])
        ld_ms[0] = c_["dsp"]
        for i in range(NT + 1):
            t, u = i, i - 1
            tp_ = t % 2
            up = u % 2
            if t + 1 < NT:
                tn = (t + 1) % 2
                dmat(xqf_s[:, tn, :, :], xqsrc[:, t + 1, :],
                     wait=[("PE", ffn1_ms[tn])])
            if t < NT:
                scol = col0 + t
                gel_ms = []
                for hc8 in range(8):
                    bk = hc8 % 2
                    for ic in range(8):
                        p.op(T, lambda h, hc8=hc8, ic=ic, bk=bk, tp_=tp_: h.matmul(
                            bank(bk), xqf_s[:, tp_, ic, :],
                            w1t_s[:, ic, hc8 * 512:(hc8 + 1) * 512],
                            start=(ic == 0), stop=(ic == 7)),
                            wait=([("dsp", ld_ms[tp_])] + bwait(bk)
                                  if ic == 0 else ()),
                            inc=[("PE", 1)] if ic == 7 else ())
                    p.op(A, lambda h, hc8=hc8, bk=bk, scol=scol: h.activation(
                        out=g1_s[:, hc8 * 512:(hc8 + 1) * 512], in_=bank(bk),
                        func=AF.Gelu, scale=sxr2[:, scol:scol + 1]),
                        wait=[("PE", None), ("DV", quant_ms.get(t - 1, 0))],
                        inc=[("AC", 1)])
                    gel_ms.append(c_["AC"])
                    bfree[bk] = ("AC", c_["AC"])
                ffn1_ms[tp_] = c_["PE"]
            if u >= 0:
                ucol = col0 + u
                for hc in range(32):
                    for half in range(2):
                        p.op(T, lambda h, hc=hc, half=half: h.matmul(
                            bank(2 + half), hqhm[:, hc, :],
                            w2t_s[:, hc, half * 512:(half + 1) * 512],
                            start=(hc == 0), stop=(hc == 31)),
                            wait=(([("dsp", hq_dm[u][hc // 4])]
                                   if half == 0 else [])
                                  + ([("dgp", w2t_ms)] + bwait(2, 3)
                                     if hc == 0 and half == 0 else [])),
                            inc=[("PE", 1)] if hc == 31 and half == 1 else ())
                ffn2_ms = c_["PE"]
                for ccc in range(8):
                    for half in range(2):
                        p.op(T, lambda h, ccc=ccc, half=half, up=up: h.matmul(
                            bank(4 + half), mxs_s[:, up, ccc, :],
                            wo_s[:, ccc, half * 512:(half + 1) * 512],
                            start=(ccc == 0), stop=(ccc == 7)),
                            wait=([("dsp", ld_ms[up]), ("dgp", None)]
                                  + bwait(4, 5)
                                  if ccc == 0 and half == 0 else ()),
                            inc=[("PE", 1)] if ccc == 7 and half == 1 else ())
                wo_ms[up] = c_["PE"]
            if t + 1 < NT:
                tn = (t + 1) % 2
                dma(mxs_s[:, tn, :, :], mxr[:, :, (t + 1) * 128:(t + 2) * 128],
                    wait=[("PE", wo_ms[tn])])
                ld_ms[tn] = c_["dsp"]
            if t < NT:
                for hc8 in range(8):
                    p.op(V, lambda h, hc8=hc8: h.tensor_reduce(
                        out=red32[:, 24 + hc8:25 + hc8],
                        in_=g1_s[:, hc8 * 512:(hc8 + 1) * 512], axis=AX,
                        op=ALU.max, apply_absolute_value=True),
                        wait=[("AC", gel_ms[hc8])], ns=True)
                p.op(V, lambda h: h.tensor_reduce(
                    out=tmpc[:, 14:15], in_=red32[:, 24:32], axis=AX,
                    op=ALU.max), ns=True)
                p.op(V, lambda h: h.tensor_scalar(
                    out=tmpc[:, 15:16], in0=tmpc[:, 14:15], scalar1=1e-5,
                    scalar2=1.0 / 127.0, op0=ALU.max, op1=ALU.mult), ns=True)
                p.op(V, lambda h: h.reciprocal(
                    out=tmpc[:, 3:4], in_=tmpc[:, 15:16]), ns=True)
                p.op(V, lambda h, tp_=tp_: h.tensor_scalar(
                    out=tmpc[:, sihc[tp_]:sihc[tp_] + 1], in0=tmpc[:, 15:16],
                    scalar1=s1c[:, 1:2], scalar2=None, op0=ALU.mult), ns=True)
                hq_dm[t] = []
                for ei in range(8):
                    sl = slice(ei * 512, (ei + 1) * 512)
                    eb = ei % 2
                    p.op(V, lambda h, sl=sl: h.tensor_scalar(
                        out=qscr[:, 0:512], in0=g1_s[:, sl], scalar1=tmpc[:, 3:4],
                        scalar2=None, op0=ALU.mult))
                    p.op(V, lambda h: h.tensor_copy(
                        out=qscr[:, 512:1024].bitcast(I32), in_=qscr[:, 0:512]))
                    p.op(V, lambda h, eb=eb: h.tensor_copy(
                        out=hq_q[:, eb, :], in_=qscr[:, 512:1024].bitcast(I32)),
                        wait=([("dsp", hqp_ms[eb])] if hqp_ms[eb] else ()),
                        inc=[("DV", 1)])
                    dmat(hqhm[:, ei * 4:(ei + 1) * 4, :], hq_q[:, eb, :],
                         wait=[("DV", None)] + ([("PE", ffn2_ms)]
                                                if ei == 0 and u >= 0 else []))
                    hqp_ms[eb] = c_["dsp"]
                    hq_dm[t].append(c_["dsp"])
                quant_ms[t] = c_["DV"]
                hq_ms[t] = c_["dsp"]
            if u >= 0:
                ucol = col0 + u
                p.op(A, lambda h, up=up: h.activation(
                    out=gt_s[:, up, 0:512], in_=bank(2), func=AF.Silu,
                    scale=tmpc[:, sihc[up]:sihc[up] + 1]),
                    wait=[("PE", ffn2_ms), ("DV", xt_free)])
                p.op(A, lambda h, up=up: h.activation(
                    out=gt_s[:, up, 512:1024], in_=bank(3), func=AF.Silu,
                    scale=tmpc[:, sihc[up]:sihc[up] + 1]), inc=[("AC", 1)])
                silu_ms = c_["AC"]
                bfree[2] = ("AC", silu_ms)
                bfree[3] = ("AC", silu_ms)
                dma(xt_s[:], xr[:, u, :], inc="dgp", eng=G,
                    wait=[("DV", xt_free)])
                p.op(V, lambda h, ucol=ucol, up=up: h.tensor_scalar(
                    out=gt_s[:, 1 - up, :], in0=xt_s[:],
                    scalar1=rstd2[:, ucol:ucol + 1], scalar2=nmr2[:, ucol:ucol + 1],
                    op0=ALU.mult, op1=ALU.add), wait=[("dgp", None)])
                for half in range(2):
                    p.op(V, lambda h, half=half, up=up: h.scalar_tensor_tensor(
                        out=st_s[:, half * 512:(half + 1) * 512],
                        in0=bank(4 + half), scalar=1.0,
                        in1=gt_s[:, 1 - up, half * 512:(half + 1) * 512],
                        op0=ALU.mult, op1=ALU.add),
                        wait=([("PE", wo_ms[up]), ("dso", st_store)]
                              if half == 0 else ()))
                bfree[4] = ("DV", c_["DV"] + 1)
                bfree[5] = ("DV", c_["DV"] + 1)
                p.op(V, lambda h, up=up: h.tensor_tensor(
                    out=gt_s[:, 1 - up, :], in0=st_s[:], in1=gt_s[:, up, :],
                    op=ALU.mult), wait=[("AC", silu_ms)])
                p.op(V, lambda h, up=up: h.tensor_tensor(
                    out=st_s[:], in0=gt_s[:, 1 - up, :], in1=xt_s[:],
                    op=ALU.add), inc=[("DV", 1)])
                xt_free = c_["DV"]
                dma(out_d[sq].rearrange("(n p) c -> p n c", p=128)[:, u, :],
                    st_s[:], inc="dso", eng=G, wait=[("DV", None)])
                st_store = c_["dso"]
        p.barrier()
        bfree = {i: None for i in range(8)}

    if DEBUG:
        dbg_d = nc.dram_tensor("dbg_s", [128, 3, 64], F32, kind="ExternalOutput").ap()
        dma(dbg_d[:, 0, :], rstd2[:], inc="dst")
        dma(dbg_d[:, 1, :], nmr2[:], inc="dst")
        dma(dbg_d[:, 2, :], sxr2[:], inc="dst")

    p.op(SY, wait=[("dst", None), ("dso", None)])

    from contextlib import ExitStack
    with ExitStack() as stk:
        block = stk.enter_context(nc.Block())
        block.sync(lambda h: [f(h) for f in p.q[SY]])
        block.tensor(lambda h: [f(h) for f in p.q[T]])
        block.vector(lambda h: [f(h) for f in p.q[V]])
        block.scalar(lambda h: [f(h) for f in p.q[A]])
        block.gpsimd(lambda h: [f(h) for f in p.q[G]])
    return nc


_NC_CACHE = None


def kernel(L0_attn_x, L0_attn_Wqkv, L0_attn_Wo, L0_attn_conv_w, L0_ffn_w1, L0_ffn_w2):
    global _NC_CACHE
    x = np.ascontiguousarray(np.asarray(L0_attn_x, dtype=np.float32))
    wqkv = np.ascontiguousarray(np.asarray(L0_attn_Wqkv, dtype=np.float32))
    wo = np.ascontiguousarray(np.asarray(L0_attn_Wo, dtype=np.float32))
    convw = np.ascontiguousarray(np.asarray(L0_attn_conv_w, dtype=np.float32))
    w1 = np.ascontiguousarray(np.asarray(L0_ffn_w1, dtype=np.float32))
    w2 = np.ascontiguousarray(np.asarray(L0_ffn_w2, dtype=np.float32))

    if _NC_CACHE is None:
        _NC_CACHE = build_nc()
    nc = _NC_CACHE
    in_maps = [{
        "x": x[core * B2:(core + 1) * B2],
        "wqkv": wqkv, "wo": wo, "convw": convw, "w1": w1, "w2": w2,
    } for core in range(8)]
    res = run_bass_kernel_spmd(nc, in_maps, core_ids=list(range(8)))
    out = np.concatenate([r["out"] for r in res.results], axis=0)
    return out.astype(np.float32)


# version 18 (final)
# speedup vs baseline: 1.0409x; 1.0031x over previous
import os
import sys

for _p in ("/opt/trn_rl_repo",):
    if os.path.isdir(_p) and _p not in sys.path:
        sys.path.insert(0, _p)

import numpy as np
import ml_dtypes

import concourse.bass as bass
import concourse.mybir as mybir
from concourse.bass_utils import run_bass_kernel_spmd

F32 = mybir.dt.float32
BF16 = mybir.dt.bfloat16
I32 = mybir.dt.int32
AF = mybir.ActivationFunctionType
ALU = mybir.AluOpType
AX = mybir.AxisListType.X

B2 = 2
S = 4096
C = 1024
H = 8
D = 128
HID = 4096
NT = 32
CSET = [0, 1, 2, 4, 8, 16]
OFFS = [0] + [2 ** k for k in range(12)]
ISQ = 0.08838834764831845
EXPB = -30.0
MAGIC = 128.5

DEBUG = False
ENGINES = ("sync", "tensor", "vector", "scalar", "gpsimd")
SY, T, V, A, G = ENGINES


def npairs(t):
    return sum(1 for c in CSET if t - c >= 0)


class Prog:

    def __init__(self, nc):
        self.nc = nc
        self.q = {e: [] for e in ENGINES}
        self.sems = {}
        self.cnt = {}
        self.last_wait = {}

    def sem(self, *names):
        for n in names:
            self.sems[n] = self.nc.alloc_semaphore(n)
            self.cnt[n] = 0

    def op(self, eng, fn=None, wait=(), inc=(), ns=False):
        if eng == "vector":
            wait = list(wait) + [("VS", self.cnt["VS"])]
            if ns:
                inc = list(inc) + [("VS", 1)]
        wl = []
        for s, v in wait:
            v = self.cnt[s] if v is None else v
            k = (eng, s)
            if v <= 0 or self.last_wait.get(k, -1) >= v:
                continue
            self.last_wait[k] = v
            wl.append((s, v))
        il = []
        for s, n in inc:
            self.cnt[s] += n
            il.append((s, n))
        sems = self.sems

        def run(h, wl=wl, il=il, fn=fn):
            for s, v in wl:
                h.wait_ge(sems[s], v)
            ins = fn(h) if fn is not None else None
            if il:
                if ins is None:
                    ins = h.nop()
                for s, n in il:
                    ins = ins.then_inc(sems[s], n)
        self.q[eng].append(run)

    def barrier(self):
        for e in ENGINES:
            if e == SY:
                self.op(e, lambda h: h.nop(),
                        wait=[(s, None) for s in ("dsp", "dst", "dgp", "dso")],
                        inc=[("stg", 1)])
            else:
                self.op(e, lambda h: h.drain(), inc=[("stg", 1)])
        cur = self.cnt["stg"]
        for e in ENGINES:
            self.op(e, wait=[("stg", cur)])


RACEDET = False


def build_nc():
    nc = bass.Bass("TRN2", target_bir_lowering=False,
                   detect_race_conditions=RACEDET)

    x_d = nc.dram_tensor("x", [B2, S, C], F32, kind="ExternalInput").ap()
    wqkv_d = nc.dram_tensor("wqkv", [C, 3 * C], F32, kind="ExternalInput").ap()
    wo_d = nc.dram_tensor("wo", [C, C], F32, kind="ExternalInput").ap()
    convw_d = nc.dram_tensor("convw", [C, 7], F32, kind="ExternalInput").ap()
    w1_d = nc.dram_tensor("w1", [HID, C], F32, kind="ExternalInput").ap()
    w2_d = nc.dram_tensor("w2", [C, HID], F32, kind="ExternalInput").ap()
    out_d = nc.dram_tensor("out", [B2, S, C], F32, kind="ExternalOutput").ap()

    _k = "ExternalOutput" if DEBUG else "Internal"
    w1qt_d = nc.dram_tensor("w1qt_s", [C, HID], BF16, kind=_k).ap()
    w2qt_d = nc.dram_tensor("w2qt_s", [HID, C], BF16, kind=_k).ap()
    xq_d = nc.dram_tensor("xq_s", [B2, S, C], BF16, kind=_k).ap()
    mix_d = nc.dram_tensor("mix_s", [B2, C, S], BF16, kind=_k).ap()

    wl_ = np.arange(128)[:, None]
    ql_ = np.arange(128)[None, :]
    masks_np = np.stack(
        [np.isin(128 * c + ql_ - wl_, OFFS).astype(np.float32) for c in CSET])
    masks_i = nc.inline_tensor(
        masks_np.transpose(1, 0, 2).astype(ml_dtypes.bfloat16).copy(), "masks_c")
    ident_i = nc.inline_tensor(np.eye(128, dtype=ml_dtypes.bfloat16), "ident_c")

    BASE = 16512

    def alloc(name, shape, dtype, offset=0):
        assert offset + BASE + int(np.prod(shape[1:])) * mybir.dt.size(dtype)             <= nc.sbuf_top, (name, offset)
        return nc.alloc_sbuf_tensor_at(name, list(shape), dtype, offset=offset + BASE)
    off = [0]

    def sal(name, shape, dtype, at=None):
        nbytes = int(np.prod(shape[1:])) * mybir.dt.size(dtype)
        if at is None:
            at = off[0]
        t = alloc(name, list(shape), dtype, offset=at)
        off[0] = max(off[0], at + ((nbytes + 31) // 32) * 32)
        return t

    masks_s = sal("masks_s", [128, 6, 128], BF16)
    ones_b = sal("ones_b", [128, 128], BF16)
    ones_f = sal("ones_f", [128, 128], F32)
    ident_s = sal("ident_s", [128, 128], BF16)
    cwf_s = sal("cwf_s", [128, H, 7], F32)
    diag_s = sal("diag_s", [128, 7, 128], BF16)
    cc = sal("cc", [128, 16], F32)
    tmpc = sal("tmpc", [128, 16], F32)
    red32 = sal("red32", [128, 32], F32)
    rstd2 = sal("rstd2", [128, 64], F32)
    nmr2 = sal("nmr2", [128, 64], F32)
    sxr2 = sal("sxr2", [128, 64], F32)
    s1c = sal("s1c", [128, 2], F32)
    assert off[0] <= 5664, off[0]
    K8 = 5664

    wfb = alloc("wfb", [128, 33, 1024], F32, offset=K8)
    wqb_s = alloc("wqb_s", [128, 32, 1024], BF16, offset=K8 + 135168)
    tstg = alloc("tstg", [128, 2, 8, 128], BF16, offset=K8 + 135168 + 65536)
    c2q = alloc("c2q", [128, 1024], BF16, offset=K8 + 135168 + 65536 + 4096)
    assert K8 + 135168 + 65536 + 4096 + 2048 <= 212832

    xn_fm = alloc("xn_fm", [128, NT, 8, 128], BF16, offset=K8)
    AT = K8 + 65536
    q_s = alloc("q_s", [128, S], BF16, offset=AT)
    k_s = alloc("k_s", [128, S], BF16, offset=AT + 8192)
    vpad = alloc("vpad", [128, 64 + S], BF16, offset=AT + 16384)
    vsT = alloc("vsT", [128, NT, 128], BF16, offset=AT + 24704)
    fq_s = alloc("fq_s", [128, S], BF16, offset=AT + 32896)
    fk_s = alloc("fk_s", [128, S], BF16, offset=AT + 41088)
    fksT = alloc("fksT", [128, NT, 128], BF16, offset=AT + 49280)
    tmpz = alloc("tmpz", [128, S], BF16, offset=AT + 57472)
    mixf = alloc("mixf", [128, S], F32, offset=AT + 65664)
    rz_s = alloc("rz_s", [128, S], F32, offset=AT + 82048)
    wexp = alloc("wexp", [128, 2, 768], BF16, offset=AT + 98432)
    rden = alloc("rden", [128, 2, 128], F32, offset=AT + 101504)
    wqkv_s = alloc("wqkv_s", [128, 2, 3, 8, 128], BF16, offset=AT + 102528)
    kv_b = alloc("kv_b", [128, 128], BF16, offset=AT + 114816)
    mixb = alloc("mixb", [128, S], BF16, offset=AT + 115104)
    wexp2 = alloc("wexp2", [128, 2, 768], BF16, offset=AT + 123296)
    zscr = alloc("zscr", [128, 512], F32, offset=AT + 126368)
    wexp4 = alloc("wexp4", [128, 4, 768], BF16, offset=AT + 128416)
    assert AT + 134560 <= 212832
    x4_s = alloc("x4_s", [128, 2, 4, 1024], F32, offset=AT)
    sqs_s = alloc("sqs_s", [128, 1024], F32, offset=AT + 32896)
    xnf_s = alloc("xnf_s", [128, 4, 1024], F32, offset=AT + 36992)
    xnb_s = alloc("xnb_s", [128, 2, 1024], BF16, offset=AT + 53376)
    xqi_s = alloc("xqi_s", [128, 1024], F32, offset=AT + 57472)
    xqb_s = alloc("xqb_s", [128, 2, 1024], BF16, offset=AT + 61568)
    assert AT + 65664 <= 212832

    w1t_s = alloc("w1t_s", [128, 8, HID], BF16, offset=K8)
    w2t_s = alloc("w2t_s", [128, 32, 1024], BF16, offset=AT)
    wo_s = alloc("wo_s", [128, 8, 1024], BF16, offset=AT + 65664)
    g1_s = alloc("g1_s", [128, HID], F32, offset=AT + 82048)
    qscr = alloc("qscr", [128, 2048], F32, offset=AT + 98432)
    hq_q = alloc("hq_q", [128, 2, 512], BF16, offset=AT + 102528)
    hqhm = alloc("hqhm", [128, 32, 128], BF16, offset=AT + 106624)
    P3L = AT + 114816
    xqf_s = alloc("xqf_s", [128, 2, 8, 128], BF16, offset=P3L)
    mxs_s = alloc("mxs_s", [128, 2, 8, 128], BF16, offset=P3L + 4096)
    xt_s = alloc("xt_s", [128, 1024], F32, offset=P3L + 8192)
    st_s = alloc("st_s", [128, 1024], F32, offset=P3L + 12288)
    gt_s = alloc("gt_s", [128, 2, 1024], F32, offset=P3L + 16384)
    assert P3L + 24576 <= 212832, P3L + 24576

    ps = nc.alloc_psum_tensor("ps", [128, 8, 512], F32)
    psf = ps.rearrange("p b n -> p (b n)")

    def bank(i, a=0, n=512):
        return ps[:, i, a:a + n]

    p = Prog(nc)
    p.sem("stg", "dsp", "dgp", "dst", "dso", "PE", "DV", "AC", "VS")
    c_ = p.cnt

    bfree = {i: None for i in range(8)}

    def bwait(*bks):
        return [bfree[b] for b in set(bks) if bfree[b] is not None]

    def dma(dst, src, inc="dsp", eng=SY, wait=()):
        p.op(eng, lambda h, d=dst, s=src: h.dma_start(out=d, in_=s),
             wait=wait, inc=[(inc, 16)])

    def dmat(dst, src, wait=()):
        p.op(SY, lambda h, d=dst, s=src: h.dma_start(out=d, in_=s, transpose=True),
             wait=wait, inc=[("dsp", 16)])

    def emit_floor(u, r1i, r1f, cbuf, out_ap, wait_last=(), inc_last=()):
        p.op(V, lambda h: h.tensor_copy(out=r1i, in_=u))
        p.op(V, lambda h: h.tensor_copy(out=r1f, in_=r1i))
        p.op(V, lambda h: h.tensor_tensor(out=cbuf, in0=r1f, in1=u, op=ALU.is_gt))
        p.op(V, lambda h: h.scalar_tensor_tensor(
            out=out_ap, in0=r1f, scalar=-128.0, in1=cbuf,
            op0=ALU.add, op1=ALU.subtract), wait=wait_last, inc=inc_last)

    dma(masks_s[:], masks_i.ap())
    dma(ident_s[:], ident_i.ap())
    dma(cwf_s[:], convw_d.rearrange("(h p) t -> p h t", p=128))
    for dst, val in ((ones_b[:], 1.0), (ones_f[:], 1.0), (cc[:, 0:1], EXPB),
                     (cc[:, 1:2], ISQ), (cc[:, 2:3], 1e-5), (cc[:, 3:4], 1e-6)):
        p.op(V, lambda h, d=dst, v=val: h.memset(d, v), inc=[("DV", 1)])

    prev_dv = None
    for wsrc, wtd, nrc, ncol, sidx in (
            (w1_d, w1qt_d, 32, 1024, 0), (w2_d, w2qt_d, 8, HID, 1)):
        if ncol == 1024:
            src_r = wsrc.rearrange("(rc p) i -> p rc i", p=128)
            for i in range(4):
                dma(wfb[:, i * 8:(i + 1) * 8, :], src_r[:, i * 8:(i + 1) * 8, :],
                    inc="dsp" if i % 2 == 0 else "dgp",
                    eng=SY if i % 2 == 0 else G)
        else:
            src_r = wsrc.rearrange("(rc p) (q i) -> p rc q i", p=128, i=1024)
            for i in range(4):
                dma(wfb[:, 0:32, :].rearrange("p (rc q) i -> p rc q i", q=4)
                    [:, i * 2:(i + 1) * 2, :, :],
                    src_r[:, i * 2:(i + 1) * 2, :, :],
                    inc="dsp" if i % 2 == 0 else "dgp",
                    eng=SY if i % 2 == 0 else G,
                    wait=[("DV", prev_dv[8 * (i + 1) - 1])])
        p.op(V, lambda h: h.tensor_reduce(
            out=red32[:, 0:32], in_=wfb[:, 0:32, :], axis=AX, op=ALU.add,
            apply_absolute_value=True),
            wait=[("dsp", None), ("dgp", None)], ns=True)
        p.op(V, lambda h: h.tensor_reduce(
            out=tmpc[:, 0:1], in_=red32[:, 0:32], axis=AX, op=ALU.add),
            inc=[("DV", 1)])
        p.op(T, lambda h: h.matmul(bank(0, 0, 1), ones_f[:], tmpc[:, 0:1],
                                   start=True, stop=True),
             wait=[("DV", None)] + bwait(0), inc=[("PE", 1)])
        sc = s1c[:, sidx:sidx + 1]
        p.op(V, lambda h, sc=sc: h.tensor_scalar(
            out=sc, in0=bank(0, 0, 1), scalar1=1.0 / (32 * 128 * 1024),
            scalar2=1e-5, op0=ALU.mult, op1=ALU.max), wait=[("PE", None)], ns=True)
        bfree[0] = ("DV", c_["DV"] + 1)
        p.op(V, lambda h, sc=sc: h.reciprocal(out=tmpc[:, 1:2], in_=sc),
             ns=True)
        st_ms = {0: 0, 1: 0}
        wtr = wtd.rearrange("(ic p) o -> p ic o", p=128)
        scr = wfb[:, 32, :]
        prev_dv = []
        for b in range(32):
            wqv = wqb_s[:, b, :]
            p.op(V, lambda h, b=b: h.tensor_scalar(
                out=scr, in0=wfb[:, b, :], scalar1=tmpc[:, 1:2],
                scalar2=None, op0=ALU.mult))
            p.op(V, lambda h: h.tensor_scalar(
                out=c2q[:], in0=scr, scalar1=-0.5,
                scalar2=None, op0=ALU.is_lt))
            p.op(V, lambda h, b=b: h.scalar_tensor_tensor(
                out=wqb_s[:, b, :], in0=scr, scalar=0.5, in1=c2q[:],
                op0=ALU.is_gt, op1=ALU.subtract), inc=[("DV", 1)])
            prev_dv.append(c_["DV"])
            rc, qq = (b, 0) if ncol == 1024 else (b // 4, b % 4)
            par = b % 2
            tv = tstg[:, par, :, :]
            dmat(tv, wqv, wait=[("DV", None), ("dst", st_ms[par])])
            dst = wtr[:, qq * 8:(qq + 1) * 8, rc * 128:(rc + 1) * 128]
            dma(dst, tv, inc="dst", wait=[("dsp", None)])
            st_ms[par] = c_["dst"]

    p.barrier()
    bfree = {i: None for i in range(8)}

    w_ms = {0: 0, 1: 0}
    for sq in range(B2):
        col0 = sq * 32
        xr = x_d[sq].rearrange("(n p) c -> p n c", p=128)
        xqr = xq_d[sq].rearrange("(n p) c -> p n c", p=128)

        grp_ms = {0: (0, 0), 1: (0, 0)}
        xqst_ms = {0: 0, 1: 0}
        xnbdm_ms = {0: 0, 1: 0}
        sqs_dv = 0
        for grp in range(8):
            gp = grp % 2
            dma(x4_s[:, gp, :, :], xr[:, grp * 4:grp * 4 + 4, :],
                inc="dgp", eng=G,
                wait=[("DV", grp_ms[gp][0]), ("AC", grp_ms[gp][1])])
            scol0 = col0 + grp * 4
            p.op(V, lambda h, gp=gp: h.tensor_reduce(
                out=red32[:, 0:4], in_=x4_s[:, gp, :, :], axis=AX, op=ALU.add),
                wait=[("dgp", None)], ns=True)
            for j in range(4):
                p.op(A, lambda h, gp=gp, j=j: h.activation(
                    out=sqs_s[:], in_=x4_s[:, gp, j, :], func=AF.Square,
                    accum_out=red32[:, 4 + j:5 + j]),
                    wait=[("dgp", None), ("DV", sqs_dv)], inc=[("AC", 1)])
            p.op(V, lambda h: h.tensor_scalar(
                out=red32[:, 8:12], in0=red32[:, 0:4], scalar1=1.0 / C,
                scalar2=None, op0=ALU.mult), wait=[("AC", None)], ns=True)
            p.op(V, lambda h: h.tensor_scalar(
                out=red32[:, 12:16], in0=red32[:, 4:8], scalar1=1.0 / C,
                scalar2=None, op0=ALU.mult), ns=True)
            p.op(V, lambda h: h.tensor_tensor(
                out=red32[:, 16:20], in0=red32[:, 8:12], in1=red32[:, 8:12],
                op=ALU.mult), ns=True)
            p.op(V, lambda h: h.tensor_tensor(
                out=red32[:, 20:24], in0=red32[:, 12:16], in1=red32[:, 16:20],
                op=ALU.subtract), inc=[("DV", 1)])
            p.op(A, lambda h: h.activation(
                out=red32[:, 24:28], in_=red32[:, 20:24], func=AF.Sqrt,
                bias=cc[:, 2:3], scale=1.0),
                wait=[("DV", None)], inc=[("AC", 1)])
            p.op(V, lambda h, scol0=scol0: h.reciprocal(
                out=rstd2[:, scol0:scol0 + 4], in_=red32[:, 24:28]),
                wait=[("AC", None)], ns=True)
            p.op(V, lambda h, scol0=scol0: h.tensor_tensor(
                out=red32[:, 28:32], in0=red32[:, 8:12],
                in1=rstd2[:, scol0:scol0 + 4], op=ALU.mult), ns=True)
            p.op(V, lambda h, scol0=scol0: h.tensor_scalar(
                out=nmr2[:, scol0:scol0 + 4], in0=red32[:, 28:32], scalar1=-1.0,
                scalar2=None, op0=ALU.mult), ns=True)
            for j in range(4):
                t = grp * 4 + j
                xt = x4_s[:, gp, j, :]
                scol = col0 + t
                p.op(V, lambda h, xt=xt, scol=scol, j=j: h.tensor_scalar(
                    out=xnf_s[:, j, :], in0=xt,
                    scalar1=rstd2[:, scol:scol + 1], scalar2=nmr2[:, scol:scol + 1],
                    op0=ALU.mult, op1=ALU.add), inc=[("DV", 1)])
                p.op(A, lambda h, xt=xt, j=j, scol=scol: h.activation(
                    out=xnb_s[:, j % 2, :], in_=xt, func=AF.Identity,
                    scale=rstd2[:, scol:scol + 1], bias=nmr2[:, scol:scol + 1]),
                    wait=[("dsp", xnbdm_ms[j % 2]), ("DV", None)],
                    inc=[("AC", 1)])
                dmat(xn_fm[:, t, :, :], xnb_s[:, j % 2, :], wait=[("AC", None)])
                xnbdm_ms[j % 2] = c_["dsp"]
                p.op(V, lambda h, j=j: h.tensor_reduce(
                    out=red32[:, 8 + j:9 + j], in_=xnf_s[:, j, :], axis=AX,
                    op=ALU.max, apply_absolute_value=True), ns=True)
            p.op(V, lambda h: h.tensor_scalar(
                out=red32[:, 12:16], in0=red32[:, 8:12], scalar1=1e-5,
                scalar2=1.0 / 127.0, op0=ALU.max, op1=ALU.mult), ns=True)
            p.op(V, lambda h: h.reciprocal(
                out=red32[:, 16:20], in_=red32[:, 12:16]), ns=True)
            p.op(V, lambda h, scol0=scol0: h.tensor_scalar(
                out=sxr2[:, scol0:scol0 + 4], in0=red32[:, 12:16],
                scalar1=s1c[:, 0:1], scalar2=None, op0=ALU.mult), ns=True)
            for j in range(4):
                t = grp * 4 + j
                p.op(V, lambda h, j=j: h.tensor_scalar(
                    out=xqi_s[:], in0=xnf_s[:, j, :],
                    scalar1=red32[:, 16 + j:17 + j],
                    scalar2=None, op0=ALU.mult))
                p.op(V, lambda h: h.tensor_copy(
                    out=sqs_s[:].bitcast(I32), in_=xqi_s[:]))
                p.op(V, lambda h, j=j: h.tensor_copy(
                    out=xqb_s[:, j % 2, :], in_=sqs_s[:].bitcast(I32)),
                    wait=[("dst", xqst_ms[j % 2])], inc=[("DV", 1)])
                dma(xqr[:, t, :], xqb_s[:, j % 2, :], inc="dst",
                    wait=[("DV", None)])
                xqst_ms[j % 2] = c_["dst"]
            sqs_dv = c_["DV"]
            grp_ms[gp] = (c_["DV"], c_["AC"])
        p.barrier()
        bfree = {i: None for i in range(8)}

        p.op(V, lambda h: h.memset(vpad[:, 58:64], 0.0), inc=[("DV", 1)])
        conv_pe_prev = 0
        kv_pe = 0
        low_pe = 0
        mixb_ms = 0
        mixst_ms = 0
        trans_ms = 0
        vsT_ms = 0
        qk_ms = 0
        mixf_ac = 0
        tmz_ms = 0
        eqms = {}
        ekms = {}
        fqms = {}
        for hh in range(H + 1):
            hv = hh - 1
            if hh < H:
                hp = hh % 2
                for pi in range(3):
                    src = wqkv_d[:, pi * C + hh * D: pi * C + hh * D + D]
                    dma(wqkv_s[:, hp, pi, :, :],
                        src.rearrange("(ic p) o -> p ic o", p=128),
                        inc="dgp", eng=G, wait=[("PE", w_ms[hp])])
                for tap in range(7):
                    p.op(V, lambda h, tap=tap, hh=hh: h.tensor_scalar(
                        out=diag_s[:, tap, :], in0=ident_s[:],
                        scalar1=cwf_s[:, hh, tap:tap + 1], scalar2=None,
                        op0=ALU.mult),
                        wait=[("PE", conv_pe_prev),
                              ("dsp", 48 if hh == 0 and sq == 0 else 0)],
                        inc=[("DV", 1)] if tap == 6 else ())
            if hv >= 0:
                for ch in range(NT):
                    p.op(T, lambda h, ch=ch: h.matmul(
                        bank(0, 0, 128), fksT[:, ch, :], vsT[:, ch, :],
                        start=(ch == 0), stop=(ch == NT - 1)),
                        wait=([("dsp", trans_ms)] + bwait(0)
                              if ch == 0 else ()),
                        inc=[("PE", 1)] if ch == NT - 1 else ())
                kv_pe = c_["PE"]
                p.op(V, lambda h: h.tensor_copy(out=kv_b[:], in_=bank(0, 0, 128)),
                     wait=[("PE", kv_pe), ("PE", low_pe)], inc=[("DV", 1)])
                bfree[0] = ("DV", c_["DV"])
            if hv >= 0:
                p.op(V, lambda h: h.tensor_scalar(
                    out=tmpz[:], in0=fq_s[:], scalar1=tmpc[:, 13:14],
                    scalar2=None, op0=ALU.mult),
                    wait=[("PE", low_pe)], inc=[("DV", 1)])
                tmz_ms = c_["DV"]
                for tc in range(8):
                    bk = 2 + tc % 2
                    p.op(T, lambda h, tc=tc, bk=bk: h.matmul(
                        bank(bk), ones_b[:], tmpz[:, tc * 512:(tc + 1) * 512],
                        start=True, stop=True),
                        wait=[("DV", tmz_ms)] + bwait(bk) if tc < 2
                        else bwait(bk),
                        inc=[("PE", 1)])
                    p.op(V, lambda h, tc=tc, bk=bk: h.tensor_scalar(
                        out=zscr[:], in0=bank(bk),
                        scalar1=1e-6, scalar2=None, op0=ALU.add),
                        wait=[("PE", None)])
                    p.op(V, lambda h, tc=tc: h.reciprocal(
                        out=rz_s[:, tc * 512:(tc + 1) * 512],
                        in_=zscr[:]), inc=[("DV", 1)])
                    bfree[bk] = ("DV", c_["DV"])
                p.op(V, lambda h: h.tensor_tensor(
                    out=tmpz[:], in0=fq_s[:], in1=rz_s[:], op=ALU.mult),
                    wait=[("PE", c_["PE"])], inc=[("DV", 1)])
                tz2_ms = c_["DV"]
            if hh < H:
                for pi, dstb in ((0, q_s), (1, k_s), (2, vpad)):
                    for tc in range(8):
                        bk = tc % 2
                        for kc in range(8):
                            p.op(T, lambda h, pi=pi, tc=tc, kc=kc, bk=bk,
                                 hp=hp: h.matmul(
                                     bank(bk), wqkv_s[:, hp, pi, kc, :],
                                     xn_fm[:, 4 * tc:4 * tc + 4, kc, :],
                                     start=(kc == 0), stop=(kc == 7)),
                                 wait=([("dgp", None)] + bwait(bk)
                                       if kc == 0 else ()),
                                 inc=[("PE", 1)] if kc == 7 else ())
                        o0 = 64 + tc * 512 if pi == 2 else tc * 512
                        p.op(A, lambda h, dstb=dstb, o0=o0, bk=bk: h.activation(
                            out=dstb[:, o0:o0 + 512], in_=bank(bk),
                            func=AF.Copy),
                            wait=[("PE", None)], inc=[("AC", 1)])
                        bfree[bk] = ("AC", c_["AC"])
                w_ms[hp] = c_["PE"]
                dmat(vsT[:], vpad[:, 64:64 + S],
                     wait=[("AC", None), ("PE", kv_pe)])
                vsT_ms = c_["dsp"]
                qk_ms = c_["AC"]
            if hv >= 0:
                for tc in range(8):
                    bk = 4 + tc % 2
                    p.op(T, lambda h, tc=tc, bk=bk: h.matmul(
                        bank(bk), kv_b[:], tmpz[:, tc * 512:(tc + 1) * 512],
                        start=True, stop=True),
                        wait=[("DV", tz2_ms)] + bwait(bk) if tc < 2
                        else bwait(bk),
                        inc=[("PE", 1)])
                    p.op(V, lambda h, tc=tc, bk=bk: h.scalar_tensor_tensor(
                        out=mixb[:, tc * 512:(tc + 1) * 512], in0=bank(bk),
                        scalar=1.0, in1=mixf[:, tc * 512:(tc + 1) * 512],
                        op0=ALU.mult, op1=ALU.add),
                        wait=[("PE", None)]
                        + ([("dst", mixst_ms), ("AC", mixf_ac)]
                           if tc == 0 else []),
                        inc=[("DV", 1)])
                    bfree[bk] = ("DV", c_["DV"])
                low_pe = c_["PE"]
                mixb_ms = c_["DV"]
                dma(mix_d[sq, hv * D:(hv + 1) * D, :], mixb[:], inc="dst",
                    wait=[("DV", None)])
                mixst_ms = c_["dst"]
            if hh < H:
                score_ms = {}
                exp_ms = {}
                mask_msd = {}
                denom_ms = {}
                norm_msd = {}
                tele_ms = {}
                low_pe0 = low_pe
                kv_pe0 = kv_pe

                def pc_eq(ck):
                    sl = slice(ck * 1024, (ck + 1) * 1024)
                    p.op(A, lambda h, sl=sl: h.activation(
                        out=tmpz[:, sl], in_=q_s[:, sl], func=AF.Exp),
                        wait=[("PE", low_pe0)] if ck == 0 else (),
                        inc=[("AC", 1)])
                    eqms[ck] = c_["AC"]

                def pc_fq(ck):
                    sl = slice(ck * 1024, (ck + 1) * 1024)
                    p.op(V, lambda h, sl=sl: h.scalar_tensor_tensor(
                        out=fq_s[:, sl], in0=q_s[:, sl], scalar=1.0,
                        in1=tmpz[:, sl], op0=ALU.add, op1=ALU.min),
                        wait=[("AC", eqms[ck])], inc=[("DV", 1)])
                    fqms[ck] = c_["DV"]

                def pc_ek(ck):
                    sl = slice(ck * 1024, (ck + 1) * 1024)
                    p.op(A, lambda h, sl=sl: h.activation(
                        out=tmpz[:, sl], in_=k_s[:, sl], func=AF.Exp),
                        wait=[("DV", fqms[ck])], inc=[("AC", 1)])
                    ekms[ck] = c_["AC"]
                    p.op(V, lambda h, sl=sl: h.scalar_tensor_tensor(
                        out=fk_s[:, sl], in0=k_s[:, sl], scalar=1.0,
                        in1=tmpz[:, sl], op0=ALU.add, op1=ALU.min),
                        wait=[("AC", ekms[ck])], inc=[("DV", 1)])
                    dmat(fksT[:, ck * 8:(ck + 1) * 8, :], fk_s[:, sl],
                         wait=[("DV", None)]
                         + ([("PE", kv_pe0)] if ck == 0 else []))

                def pc_fksum():
                    for ck in range(4):
                        sl = slice(ck * 1024, (ck + 1) * 1024)
                        p.op(V, lambda h, ck=ck, sl=sl: h.tensor_reduce(
                            out=red32[:, 16 + ck:17 + ck], in_=fk_s[:, sl],
                            axis=AX, op=ALU.add), ns=True)
                    p.op(V, lambda h: h.tensor_reduce(
                        out=tmpc[:, 13:14], in_=red32[:, 16:20], axis=AX,
                        op=ALU.add), ns=True)

                pieces = ([lambda ck=ck: pc_eq(ck) for ck in range(4)]
                          + [lambda ck=ck: pc_fq(ck) for ck in range(4)]
                          + [lambda ck=ck: pc_ek(ck) for ck in range(4)]
                          + [pc_fksum])
                for it in range(NT + 2):
                    t, u, v = it, it - 1, it - 2
                    if t < NT:
                        par = t % 2
                        np_t = npairs(t)
                        sb = 2 + 2 * par
                        nsb = min(np_t, 4)
                        for ci in range(np_t):
                            w = t - CSET[ci]
                            bnk = sb if ci < 4 else sb + 1
                            a0 = (ci % 4) * 128
                            first = (ci == 0) or (ci == 4)
                            last = (ci == nsb - 1) or (ci == np_t - 1)
                            p.op(T, lambda h, w=w, t=t, bnk=bnk, a0=a0,
                                 f=first, l=last: h.matmul(
                                     bank(bnk, a0, 128),
                                     k_s[:, w * 128:(w + 1) * 128],
                                     q_s[:, t * 128:(t + 1) * 128],
                                     start=f, stop=l),
                                 wait=(([("AC", qk_ms)] if t < 2 else [])
                                       + bwait(sb, sb + 1)) if ci == 0 else (),
                                 inc=[("PE", 1)] if ci == np_t - 1 else ())
                        score_ms[t] = c_["PE"]
                    if 0 <= u < NT:
                        par_u = u % 2
                        np_u = npairs(u)
                        sbu = 2 + 2 * par_u
                        for ci in range(np_u):
                            p.op(T, lambda h, ci=ci, par_u=par_u, sbu=sbu,
                                 np_u=np_u: h.matmul(
                                     bank(sbu + 1, 256, 128), ones_b[:],
                                     wexp2[:, par_u, ci * 128:(ci + 1) * 128],
                                     start=(ci == 0), stop=(ci == np_u - 1)),
                                 wait=([("DV", mask_msd[u])] + bwait(sbu + 1)
                                       if ci == 0 else ()),
                                 inc=[("PE", 1)] if ci == np_u - 1 else ())
                        denom_ms[u] = c_["PE"]
                    if 0 <= v < NT:
                        par_v = v % 2
                        np_v = npairs(v)
                        dtv = 6 + par_v
                        for ci in range(np_v):
                            w = v - CSET[ci]
                            p.op(T, lambda h, ci=ci, w=w, v=v,
                                 dtv=dtv: h.matmul(
                                     bank(dtv, 0, 128), vsT[:, w, :],
                                     wexp4[:, v % 4, ci * 128:(ci + 1) * 128],
                                     start=(ci == 0), stop=False),
                                 wait=[("DV", norm_msd[v]), ("dsp", vsT_ms)]
                                 + bwait(dtv) if ci == 0 else ())
                        for tap in range(7):
                            p.op(T, lambda h, tap=tap, v=v, dtv=dtv: h.matmul(
                                bank(dtv, 0, 128), diag_s[:, tap, :],
                                vpad[:, 58 + tap + v * 128:
                                     58 + tap + v * 128 + 128],
                                start=False, stop=(tap == 6)),
                                inc=[("PE", 1)] if tap == 6 else ())
                        tele_ms[v] = c_["PE"]
                    if t < NT:
                        p.op(A, lambda h, np_t=np_t, sb=sb, t=t: h.activation(
                            out=wexp4[:, t % 4, 0:np_t * 128],
                            in_=psf[:, sb * 512: sb * 512 + np_t * 128],
                            func=AF.Exp, bias=cc[:, 0:1], scale=cc[:, 1:2]),
                            wait=[("PE", score_ms[t]),
                                  ("PE", tele_ms.get(t - 4, conv_pe_prev))],
                            inc=[("AC", 1)])
                        exp_ms[t] = c_["AC"]
                        bfree[sb] = ("AC", exp_ms[t])
                    if 0 <= v < NT:
                        p.op(A, lambda h, v=v, dtv=dtv: h.activation(
                            out=mixf[:, v * 128:(v + 1) * 128],
                            in_=bank(dtv, 0, 128), func=AF.Copy),
                            wait=[("PE", tele_ms[v])]
                            + ([("DV", mixb_ms)] if v == 0 else []),
                            inc=[("AC", 1)])
                        bfree[dtv] = ("AC", c_["AC"])
                    if t < NT:
                        p.op(V, lambda h, np_t=np_t, t=t: h.tensor_tensor(
                            out=wexp2[:, t % 2, 0:np_t * 128],
                            in0=wexp4[:, t % 4, 0:np_t * 128],
                            in1=masks_s[:, 0:np_t, :]
                            .rearrange("p a b -> p (a b)"),
                            op=ALU.mult),
                            wait=[("AC", exp_ms[t]),
                                  ("PE", denom_ms.get(t - 2, conv_pe_prev))],
                            inc=[("DV", 1)])
                        mask_msd[t] = c_["DV"]
                    if 0 <= u < NT:
                        p.op(V, lambda h, par_u=par_u, sbu=sbu: h.reciprocal(
                            out=rden[:, par_u, :],
                            in_=bank(sbu + 1, 256, 128)),
                            wait=[("PE", denom_ms[u])], inc=[("DV", 1)])
                        bfree[sbu + 1] = ("DV", c_["DV"])
                        for ci in range(np_u):
                            p.op(V, lambda h, ci=ci, u=u,
                                 par_u=par_u: h.tensor_tensor(
                                     out=wexp4[:, u % 4,
                                               ci * 128:(ci + 1) * 128],
                                     in0=wexp2[:, par_u,
                                               ci * 128:(ci + 1) * 128],
                                     in1=rden[:, par_u, :], op=ALU.mult),
                                 inc=[("DV", 1)] if ci == np_u - 1 else ())
                        norm_msd[u] = c_["DV"]
                    if 2 <= it < 2 + len(pieces):
                        pieces[it - 2]()
                conv_pe_prev = c_["PE"]
                trans_ms = c_["dsp"]
                mixf_ac = c_["AC"]
        p.barrier()
        bfree = {i: None for i in range(8)}

        w1r = w1qt_d.rearrange("(ic p) o -> p ic o", p=128)
        dma(w1t_s[:, 0:4, :], w1r[:, 0:4, :])
        dma(w1t_s[:, 4:8, :], w1r[:, 4:8, :], inc="dgp", eng=G)
        w1g_ms = c_["dgp"]
        dma(w2t_s[:], w2qt_d.rearrange("(hc p) o -> p hc o", p=128),
            inc="dgp", eng=G)
        w2t_ms = c_["dgp"]
        dma(wo_s[:], wo_d.rearrange("(cc p) o -> p cc o", p=128),
            inc="dgp", eng=G)
        mxr = mix_d[sq].rearrange("(cc p) n -> p cc n", p=128)
        xqsrc = xq_d[sq].rearrange("(n p) c -> p n c", p=128)
        ffn1_ms = {0: 0, 1: 0}
        wo_ms = {0: 0, 1: 0}
        ld_ms = {0: 0, 1: 0}
        hq_ms = {}
        hq_dm = {}
        hqp_ms = {0: 0, 1: 0}
        quant_ms = {}
        sihc = {0: 12, 1: 13}
        ffn2_ms = 0
        silu_ms = 0
        xt_free = 0
        st_store = 0
        dmat(xqf_s[:, 0, :, :], xqsrc[:, 0, :])
        dma(mxs_s[:, 0, :, :], mxr[:, :, 0:128])
        ld_ms[0] = c_["dsp"]
        for i in range(NT + 1):
            t, u = i, i - 1
            tp_ = t % 2
            up = u % 2
            if t + 1 < NT:
                tn = (t + 1) % 2
                dmat(xqf_s[:, tn, :, :], xqsrc[:, t + 1, :],
                     wait=[("PE", ffn1_ms[tn])])
            if t < NT:
                scol = col0 + t
                gel_ms = []
                for hc8 in range(8):
                    bk = hc8 % 2
                    for ic in range(8):
                        p.op(T, lambda h, hc8=hc8, ic=ic, bk=bk, tp_=tp_: h.matmul(
                            bank(bk), xqf_s[:, tp_, ic, :],
                            w1t_s[:, ic, hc8 * 512:(hc8 + 1) * 512],
                            start=(ic == 0), stop=(ic == 7)),
                            wait=([("dsp", ld_ms[tp_]), ("dgp", w1g_ms)]
                                  + bwait(bk)
                                  if ic == 0 else ()),
                            inc=[("PE", 1)] if ic == 7 else ())
                    p.op(A, lambda h, hc8=hc8, bk=bk, scol=scol: h.activation(
                        out=g1_s[:, hc8 * 512:(hc8 + 1) * 512], in_=bank(bk),
                        func=AF.Gelu, scale=sxr2[:, scol:scol + 1]),
                        wait=[("PE", None), ("DV", quant_ms.get(t - 1, 0))],
                        inc=[("AC", 1)])
                    gel_ms.append(c_["AC"])
                    bfree[bk] = ("AC", c_["AC"])
                ffn1_ms[tp_] = c_["PE"]
            if u >= 0:
                ucol = col0 + u
                for hc in range(32):
                    for half in range(2):
                        p.op(T, lambda h, hc=hc, half=half: h.matmul(
                            bank(2 + half), hqhm[:, hc, :],
                            w2t_s[:, hc, half * 512:(half + 1) * 512],
                            start=(hc == 0), stop=(hc == 31)),
                            wait=(([("dsp", hq_dm[u][hc // 4])]
                                   if half == 0 else [])
                                  + ([("dgp", w2t_ms)] + bwait(2, 3)
                                     if hc == 0 and half == 0 else [])),
                            inc=[("PE", 1)] if hc == 31 and half == 1 else ())
                ffn2_ms = c_["PE"]
                for ccc in range(8):
                    for half in range(2):
                        p.op(T, lambda h, ccc=ccc, half=half, up=up: h.matmul(
                            bank(4 + half), mxs_s[:, up, ccc, :],
                            wo_s[:, ccc, half * 512:(half + 1) * 512],
                            start=(ccc == 0), stop=(ccc == 7)),
                            wait=([("dsp", ld_ms[up]), ("dgp", None)]
                                  + bwait(4, 5)
                                  if ccc == 0 and half == 0 else ()),
                            inc=[("PE", 1)] if ccc == 7 and half == 1 else ())
                wo_ms[up] = c_["PE"]
            if t + 1 < NT:
                tn = (t + 1) % 2
                dma(mxs_s[:, tn, :, :], mxr[:, :, (t + 1) * 128:(t + 2) * 128],
                    wait=[("PE", wo_ms[tn])])
                ld_ms[tn] = c_["dsp"]
            if t < NT:
                for hc8 in range(8):
                    p.op(V, lambda h, hc8=hc8: h.tensor_reduce(
                        out=red32[:, 24 + hc8:25 + hc8],
                        in_=g1_s[:, hc8 * 512:(hc8 + 1) * 512], axis=AX,
                        op=ALU.max, apply_absolute_value=True),
                        wait=[("AC", gel_ms[hc8])], ns=True)
                p.op(V, lambda h: h.tensor_reduce(
                    out=tmpc[:, 14:15], in_=red32[:, 24:32], axis=AX,
                    op=ALU.max), ns=True)
                p.op(V, lambda h: h.tensor_scalar(
                    out=tmpc[:, 15:16], in0=tmpc[:, 14:15], scalar1=1e-5,
                    scalar2=1.0 / 127.0, op0=ALU.max, op1=ALU.mult), ns=True)
                p.op(V, lambda h: h.reciprocal(
                    out=tmpc[:, 3:4], in_=tmpc[:, 15:16]), ns=True)
                p.op(V, lambda h, tp_=tp_: h.tensor_scalar(
                    out=tmpc[:, sihc[tp_]:sihc[tp_] + 1], in0=tmpc[:, 15:16],
                    scalar1=s1c[:, 1:2], scalar2=None, op0=ALU.mult), ns=True)
                hq_dm[t] = []
                for ei in range(8):
                    sl = slice(ei * 512, (ei + 1) * 512)
                    eb = ei % 2
                    p.op(V, lambda h, sl=sl: h.tensor_scalar(
                        out=qscr[:, 0:512], in0=g1_s[:, sl], scalar1=tmpc[:, 3:4],
                        scalar2=None, op0=ALU.mult))
                    p.op(V, lambda h: h.tensor_copy(
                        out=qscr[:, 512:1024].bitcast(I32), in_=qscr[:, 0:512]))
                    p.op(V, lambda h, eb=eb: h.tensor_copy(
                        out=hq_q[:, eb, :], in_=qscr[:, 512:1024].bitcast(I32)),
                        wait=([("dsp", hqp_ms[eb])] if hqp_ms[eb] else ()),
                        inc=[("DV", 1)])
                    dmat(hqhm[:, ei * 4:(ei + 1) * 4, :], hq_q[:, eb, :],
                         wait=[("DV", None)] + ([("PE", ffn2_ms)]
                                                if ei == 0 and u >= 0 else []))
                    hqp_ms[eb] = c_["dsp"]
                    hq_dm[t].append(c_["dsp"])
                quant_ms[t] = c_["DV"]
                hq_ms[t] = c_["dsp"]
            if u >= 0:
                ucol = col0 + u
                p.op(A, lambda h, up=up: h.activation(
                    out=gt_s[:, up, 0:512], in_=bank(2), func=AF.Silu,
                    scale=tmpc[:, sihc[up]:sihc[up] + 1]),
                    wait=[("PE", ffn2_ms), ("DV", xt_free)])
                p.op(A, lambda h, up=up: h.activation(
                    out=gt_s[:, up, 512:1024], in_=bank(3), func=AF.Silu,
                    scale=tmpc[:, sihc[up]:sihc[up] + 1]), inc=[("AC", 1)])
                silu_ms = c_["AC"]
                bfree[2] = ("AC", silu_ms)
                bfree[3] = ("AC", silu_ms)
                dma(xt_s[:], xr[:, u, :], inc="dgp", eng=G,
                    wait=[("DV", xt_free)])
                p.op(V, lambda h, ucol=ucol, up=up: h.tensor_scalar(
                    out=gt_s[:, 1 - up, :], in0=xt_s[:],
                    scalar1=rstd2[:, ucol:ucol + 1], scalar2=nmr2[:, ucol:ucol + 1],
                    op0=ALU.mult, op1=ALU.add), wait=[("dgp", None)])
                for half in range(2):
                    p.op(V, lambda h, half=half, up=up: h.scalar_tensor_tensor(
                        out=st_s[:, half * 512:(half + 1) * 512],
                        in0=bank(4 + half), scalar=1.0,
                        in1=gt_s[:, 1 - up, half * 512:(half + 1) * 512],
                        op0=ALU.mult, op1=ALU.add),
                        wait=([("PE", wo_ms[up]), ("dso", st_store)]
                              if half == 0 else ()))
                bfree[4] = ("DV", c_["DV"] + 1)
                bfree[5] = ("DV", c_["DV"] + 1)
                p.op(V, lambda h, up=up: h.tensor_tensor(
                    out=gt_s[:, 1 - up, :], in0=st_s[:], in1=gt_s[:, up, :],
                    op=ALU.mult), wait=[("AC", silu_ms)])
                p.op(V, lambda h, up=up: h.tensor_tensor(
                    out=st_s[:], in0=gt_s[:, 1 - up, :], in1=xt_s[:],
                    op=ALU.add), inc=[("DV", 1)])
                xt_free = c_["DV"]
                dma(out_d[sq].rearrange("(n p) c -> p n c", p=128)[:, u, :],
                    st_s[:], inc="dso", eng=G, wait=[("DV", None)])
                st_store = c_["dso"]
        p.barrier()
        bfree = {i: None for i in range(8)}

    if DEBUG:
        dbg_d = nc.dram_tensor("dbg_s", [128, 3, 64], F32, kind="ExternalOutput").ap()
        dma(dbg_d[:, 0, :], rstd2[:], inc="dst")
        dma(dbg_d[:, 1, :], nmr2[:], inc="dst")
        dma(dbg_d[:, 2, :], sxr2[:], inc="dst")

    p.op(SY, wait=[("dst", None), ("dso", None)])

    from contextlib import ExitStack
    with ExitStack() as stk:
        block = stk.enter_context(nc.Block())
        block.sync(lambda h: [f(h) for f in p.q[SY]])
        block.tensor(lambda h: [f(h) for f in p.q[T]])
        block.vector(lambda h: [f(h) for f in p.q[V]])
        block.scalar(lambda h: [f(h) for f in p.q[A]])
        block.gpsimd(lambda h: [f(h) for f in p.q[G]])
    return nc


_NC_CACHE = None


def kernel(L0_attn_x, L0_attn_Wqkv, L0_attn_Wo, L0_attn_conv_w, L0_ffn_w1, L0_ffn_w2):
    global _NC_CACHE
    x = np.ascontiguousarray(np.asarray(L0_attn_x, dtype=np.float32))
    wqkv = np.ascontiguousarray(np.asarray(L0_attn_Wqkv, dtype=np.float32))
    wo = np.ascontiguousarray(np.asarray(L0_attn_Wo, dtype=np.float32))
    convw = np.ascontiguousarray(np.asarray(L0_attn_conv_w, dtype=np.float32))
    w1 = np.ascontiguousarray(np.asarray(L0_ffn_w1, dtype=np.float32))
    w2 = np.ascontiguousarray(np.asarray(L0_ffn_w2, dtype=np.float32))

    if _NC_CACHE is None:
        _NC_CACHE = build_nc()
    nc = _NC_CACHE
    in_maps = [{
        "x": x[core * B2:(core + 1) * B2],
        "wqkv": wqkv, "wo": wo, "convw": convw, "w1": w1, "w2": w2,
    } for core in range(8)]
    res = run_bass_kernel_spmd(nc, in_maps, core_ids=list(range(8)))
    out = np.concatenate([r["out"] for r in res.results], axis=0)
    return out.astype(np.float32)
